# version 14
# baseline (speedup 1.0000x reference)
import numpy as np
from contextlib import ExitStack

import concourse.bass as bass
import concourse.mybir as mybir
from concourse.bass_utils import run_bass_kernel_spmd

F32 = mybir.dt.float32
BF16 = mybir.dt.bfloat16
AF = mybir.ActivationFunctionType
AX = mybir.AxisListType
ALU = mybir.AluOpType

D = 2048
NT = 8192
OWN = 1024
NH = 16
NP = 2
HP = 8
NDC = 16
NSTR = 16
STR = 512
NTILE = 64
NHB = 64
VSLOT = 132

INVD = 1.0 / 2048.0
EPS = 1e-5
QKSCALE = 0.08838834764831845
CSHIFT = 4.0


def build_nc():
    nc = bass.Bass("TRN2", target_bir_lowering=False)

    x = nc.dram_tensor("x", [NT, D], F32, kind="ExternalInput")
    x_own = nc.dram_tensor("x_own", [OWN, D], F32, kind="ExternalInput")
    wqp = nc.dram_tensor("wqp", [NP, 128, HP * 2048], F32, kind="ExternalInput")
    wkp = nc.dram_tensor("wkp", [NP, 128, HP * 2048], F32, kind="ExternalInput")
    wvp = nc.dram_tensor("wvp", [NP, 128, HP * 2048], F32, kind="ExternalInput")
    wgb = nc.dram_tensor("wgb", [NHB, 128, 2048], F32, kind="ExternalInput")
    wub = nc.dram_tensor("wub", [NHB, 128, 2048], F32, kind="ExternalInput")
    wdq = nc.dram_tensor("wdq", [4, 64, 128, 512], F32, kind="ExternalInput")
    wpq = nc.dram_tensor("wpq", [4, 128, 8192], F32, kind="ExternalInput")
    biasb = nc.dram_tensor("biasb", [128, D], F32, kind="ExternalInput")
    out = nc.dram_tensor("out", [OWN, D], F32, kind="ExternalOutput")

    xn_nat = nc.dram_tensor("xn_nat", [NT, D], BF16, kind="Internal")
    wkb = nc.dram_tensor("wkb", [NP, 128, HP * 2048], BF16, kind="Internal")
    wvb = nc.dram_tensor("wvb", [NP, 128, HP * 2048], BF16, kind="Internal")
    wqb = nc.dram_tensor("wqb", [NP, 128, HP * 2048], BF16, kind="Internal")
    xn_own_nat = nc.dram_tensor("xn_own_nat", [OWN, D], BF16, kind="Internal")
    kT_scr = nc.dram_tensor("kT_scr", [NH, 128, NT], BF16, kind="Internal")
    v_scr = nc.dram_tensor("v_scr", [NT, NH, 128], BF16, kind="Internal")
    ctx_nat = nc.dram_tensor("ctx_nat", [OWN, D], BF16, kind="Internal")
    gated_scr = nc.dram_tensor("gated_scr", [NHB, 128, OWN], BF16, kind="Internal")

    st = ExitStack()
    with st:
        xnT_own = st.enter_context(nc.sbuf_tensor("xnT_own", [128, NDC * OWN], BF16))
        qT = st.enter_context(nc.sbuf_tensor("qT", [128, NH * OWN], BF16))
        cst = st.enter_context(nc.sbuf_tensor("cst", [128, 8], F32))
        pS = [st.enter_context(nc.psum_tensor(f"pS{i}", [128, 1024], F32)) for i in range(2)]
        pC = [st.enter_context(nc.psum_tensor(f"pC{i}", [128, 512], F32)) for i in range(4)]

        def sem(name):
            return st.enter_context(nc.semaphore(name))

        sXin = [sem("sXin0"), sem("sXin1")]
        sSq = sem("sSq")
        sVar = sem("sVar")
        sStd = sem("sStd")
        sNR = sem("sNR")
        sLN = sem("sLN")
        sXW = [sem(f"sXW{i}") for i in range(3)]
        sTOwn = sem("sTOwn")
        sTR = [sem("sTR0"), sem("sTR1")]
        sPEx = sem("sPEx")
        sWk = sem("sWk")
        sWv = sem("sWv")
        sWq = sem("sWq")
        sWkB = sem("sWkB")
        sWvB = sem("sWvB")
        sWqB = sem("sWqB")
        sPEkvp = sem("sPEkvp")
        sDVkv = sem("sDVkv")
        sKW = [sem("sKW0"), sem("sKW1")]
        sPEqs = sem("sPEqs")
        sDVq = sem("sDVq")
        sKL = [sem("sKL0"), sem("sKL1")]
        sVL = [sem("sVL0"), sem("sVL1")]
        sS = sem("sS")
        sEXP = [sem("sEXP0"), sem("sEXP1")]
        sPVd = sem("sPVd")
        sPEpv = sem("sPEpv")
        sDVctx = sem("sDVctx")
        sDVN = sem("sDVN")
        sCTXW = [sem("sCTXW0"), sem("sCTXW1")]
        sWgu = [sem("sWgu0"), sem("sWgu1")]
        sPEg = sem("sPEg")
        sSil = sem("sSil")
        sDVgu = sem("sDVgu")
        sGW = [sem("sGW0"), sem("sGW1")]
        sTctx = sem("sTctx")
        sTctx2 = sem("sTctx2")
        sWp = [sem("sWp0"), sem("sWp1")]
        sGin = [sem(f"sGin{i}") for i in range(3)]
        sGR = [sem(f"sGR{i}") for i in range(3)]
        sPEhc = sem("sPEhc")
        sPEob = sem("sPEob")
        sXO = [sem(f"sXO{i}") for i in range(4)]
        sDVo = sem("sDVo")
        sOUT = [sem("sOUT0"), sem("sOUT1")]
        sBB = sem("sBB")
        sINIT = sem("sINIT")
        sWC = sem("sWC")

        with ExitStack() as sa:
            xinp = [sa.enter_context(nc.sbuf_tensor(f"xinp{i}", [128, 2 * D], BF16)) for i in range(2)]
            stats = [sa.enter_context(nc.sbuf_tensor(f"stats{i}", [128, 16], F32)) for i in range(4)]
            xn_bf = [sa.enter_context(nc.sbuf_tensor(f"xnbf{i}", [128, D], BF16)) for i in range(3)]
            xnTb = [sa.enter_context(nc.sbuf_tensor(f"xnTb{i}", [128, NDC * STR], BF16,
                                                    is_transpose_dest=True)) for i in range(2)]
            wk_sb = sa.enter_context(nc.sbuf_tensor("wk_sb", [128, HP * 2048], BF16))
            wv_sb = sa.enter_context(nc.sbuf_tensor("wv_sb", [128, HP * 2048], BF16))
            kst = [sa.enter_context(nc.sbuf_tensor(f"kst{i}", [128, 512], BF16)) for i in range(2)]
            vst = kst

            blkA = sa.enter_context(nc.Block())

            def whalf(ap, hf):
                return ap.rearrange("p (c w) -> p c w", w=1024)[:, :, hf * 512:(hf + 1) * 512]

            @blkA.gpsimd
            def _(g):
                def x_pair(j):
                    jp = j % 2
                    if j >= 2:
                        g.wait_ge(sLN, 2 * j - 2)
                    g.dma_start(xinp[jp][:, :].rearrange("p (s f) -> p s f", s=2),
                                x[j * 256:(j + 1) * 256, :].rearrange("(s p) f -> p s f", p=128)
                                ).then_inc(sXin[jp], 16)

                x_pair(0)
                x_pair(1)
                g.dma_start(whalf(wk_sb[:, :], 0), whalf(wkp[0], 0)).then_inc(sWk, 16)
                x_pair(2)
                g.dma_start(whalf(wk_sb[:, :], 1), whalf(wkp[0], 1)).then_inc(sWkB, 16)
                x_pair(3)
                g.dma_start(whalf(wv_sb[:, :], 0), whalf(wvp[0], 0)).then_inc(sWv, 16)
                x_pair(4)
                g.dma_start(whalf(wv_sb[:, :], 1), whalf(wvp[0], 1)).then_inc(sWvB, 16)
                for j in range(5, NTILE // 2):
                    x_pair(j)
                g.dma_start(wqb[0, :, :], wqp[0, :, :]).then_inc(sWC, 16)
                for pa in range(1, NP):
                    g.dma_start(wkb[pa, :, :], wkp[pa, :, :]).then_inc(sWC, 16)
                    g.dma_start(wvb[pa, :, :], wvp[pa, :, :]).then_inc(sWC, 16)
                    g.dma_start(wqb[pa, :, :], wqp[pa, :, :]).then_inc(sWC, 16)
                g.wait_ge(sWC, 16 * (3 * NP - 2))
                for pa in range(NP):
                    g.wait_ge(sPEkvp, 15 * (NSTR * (pa + 1) - 1) + 8)
                    g.dma_start(whalf(wk_sb[:, :], 0), whalf(wqb[pa], 0)).then_inc(sWq, 16)
                    g.dma_start(whalf(wk_sb[:, :], 1), whalf(wqb[pa], 1)).then_inc(sWqB, 16)
                    if pa + 1 < NP:
                        g.wait_ge(sPEx, NSTR * (pa + 1))
                        g.dma_start(whalf(wv_sb[:, :], 0), whalf(wvb[pa + 1], 0)).then_inc(sWv, 16)
                        g.dma_start(whalf(wv_sb[:, :], 1), whalf(wvb[pa + 1], 1)).then_inc(sWvB, 16)
                        g.wait_ge(sPEqs, 16 * pa + 8)
                        g.dma_start(whalf(wk_sb[:, :], 0), whalf(wkb[pa + 1], 0)).then_inc(sWk, 16)
                        g.wait_ge(sPEqs, 16 * (pa + 1))
                        g.dma_start(whalf(wk_sb[:, :], 1), whalf(wkb[pa + 1], 1)).then_inc(sWkB, 16)

            @blkA.vector
            def _(v):
                v.memset(cst[:, 0:1], EPS)
                v.memset(cst[:, 1:2], QKSCALE)
                v.memset(cst[:, 2:3], -CSHIFT)
                v.memset(cst[:, 3:4], INVD)
                v.memset(cst[:, 4:5], -1.0)
                v.memset(cst[:, 5:6], 1.0).then_inc(sINIT, 1)

                def ln_tile(t):
                    sl = t % 4
                    S = stats[sl]
                    xv = xinp[(t // 2) % 2][:, (t % 2) * D:(t % 2 + 1) * D]
                    v.wait_ge(sXin[(t // 2) % 2], 16 * (t // 4 + 1))
                    v.reduce_sum(S[:, 0:1], xv, axis=AX.X)
                    v.drain()
                    v.wait_ge(sSq, t + 1)
                    v.tensor_scalar_mul(S[:, 2:3], S[:, 0:1], cst[:, 3:4])
                    v.drain()
                    v.tensor_mul(S[:, 3:4], S[:, 2:3], S[:, 2:3])
                    v.drain()
                    v.scalar_tensor_tensor(S[:, 4:5], S[:, 1:2], cst[:, 3:4], S[:, 3:4],
                                           ALU.mult, ALU.subtract).then_inc(sVar, 1)
                    v.wait_ge(sStd, t + 1)
                    v.reciprocal(S[:, 6:7], S[:, 5:6])
                    v.drain()
                    v.scalar_tensor_tensor(S[:, 7:8], S[:, 2:3], cst[:, 4:5], S[:, 6:7],
                                           ALU.mult, ALU.mult).then_inc(sNR, 1)
                    v.drain()
                    v.tensor_scalar(xn_bf[t % 3][:, :], xv, S[:, 6:7], S[:, 7:8],
                                    ALU.mult, ALU.add).then_inc(sLN, 1)

                def kv_evac_stripe(pa, s):
                    S2 = pa * NSTR + s
                    for kb in range(HP):
                        n = 16 * S2 + kb
                        v.wait_ge(sPEkvp, 15 * S2 + kb + 1)
                        if n >= 2:
                            v.wait_ge(sKW[n % 2], 16 * (n // 2))
                        v.tensor_copy(kst[n % 2][:, :], pC[kb % 2][:, :]).then_inc(sDVkv, 1)
                    for hv in range(HP):
                        n = 16 * S2 + HP + hv
                        if hv < HP - 1:
                            v.wait_ge(sPEkvp, 15 * S2 + HP + hv + 1)
                        else:
                            v.wait_ge(sPEx, S2 + 1)
                        v.wait_ge(sKW[n % 2], 16 * (n // 2))
                        v.tensor_copy(vst[n % 2][:, :], pC[2 + hv % 2][:, :]).then_inc(sDVkv, 1)

                def q_evac(pa):
                    for qb in range(16):
                        gq = 16 * pa + qb
                        v.wait_ge(sPEqs, gq + 1)
                        h = pa * HP + qb // 2
                        ist = qb % 2
                        v.tensor_copy(qT[:, h * OWN + ist * STR: h * OWN + (ist + 1) * STR],
                                      pC[qb % 2][:, :]).then_inc(sDVq, 1)

                LOOK = 12
                for tt in range(LOOK):
                    ln_tile(tt)
                for s in range(NSTR):
                    kv_evac_stripe(0, s)
                    for tt in range(4 * s + LOOK, min(4 * s + LOOK + 4, NTILE)):
                        ln_tile(tt)
                q_evac(0)
                for pa in range(1, NP):
                    for s in range(NSTR):
                        kv_evac_stripe(pa, s)
                    q_evac(pa)

            @blkA.scalar
            def _(sc):
                sc.wait_ge(sINIT, 1)

                def ln_acts(t):
                    sl = t % 4
                    S = stats[sl]
                    xv = xinp[(t // 2) % 2][:, (t % 2) * D:(t % 2 + 1) * D]
                    sc.wait_ge(sXin[(t // 2) % 2], 16 * (t // 4 + 1))
                    if t >= 4:
                        sc.wait_ge(sLN, t - 3)
                    if t >= 3:
                        sc.wait_ge(sLN, t - 2)
                        sc.wait_ge(sXW[t % 3], 16 * (t // 3))
                    sc.drain()
                    sc.activation(xn_bf[t % 3][:, :], xv, AF.Square,
                                  accum_out=S[:, 1:2]).then_inc(sSq, 1)
                    sc.wait_ge(sVar, t + 1)
                    sc.activation(S[:, 5:6], S[:, 4:5], AF.Sqrt,
                                  bias=cst[:, 0:1]).then_inc(sStd, 1)

                def kv_write_stripe(pa, s):
                    S2 = pa * NSTR + s
                    for kb in range(HP):
                        n = 16 * S2 + kb
                        sc.wait_ge(sDVkv, n + 1)
                        if n >= 2:
                            sc.wait_ge(sKW[n % 2], 16 * (n // 2))
                        h = pa * HP + kb
                        sc.dma_start(kT_scr[h, :, s * STR:(s + 1) * STR],
                                     kst[n % 2][:, :]).then_inc(sKW[n % 2], 16)
                    for hv in range(HP):
                        n = 16 * S2 + HP + hv
                        tb, half = hv // 2, hv % 2
                        sc.wait_ge(sDVkv, n + 1)
                        sc.wait_ge(sKW[n % 2], 16 * (n // 2))
                        dst = v_scr[s * STR + tb * 128: s * STR + (tb + 1) * 128,
                                    pa * HP + half * 4: pa * HP + (half + 1) * 4, :]
                        src = vst[n % 2][:, :].rearrange("p (h c) -> p h c", c=128)
                        sc.dma_start(dst, src).then_inc(sKW[n % 2], 16)

                LOOK = 12
                for tt in range(LOOK):
                    ln_acts(tt)
                for s in range(NSTR):
                    kv_write_stripe(0, s)
                    for tt in range(4 * s + LOOK, min(4 * s + LOOK + 4, NTILE)):
                        ln_acts(tt)
                for pa in range(1, NP):
                    for s in range(NSTR):
                        kv_write_stripe(pa, s)

            @blkA.sync
            def _(sy):
                for t in range(NTILE):
                    sl = t % 3
                    sy.wait_ge(sLN, t + 1)
                    if t >= 3:
                        sy.wait_ge(sXW[sl], 16 * (t // 3))
                    sy.dma_start(xn_nat[t * 128:(t + 1) * 128, :], xn_bf[sl][:, :]).then_inc(sXW[sl], 16)
                    if t % 4 == 3 and t >= 4:
                        ss = t // 4 - 1
                        sp = ss % 2
                        for sl2 in range(3):
                            sy.wait_ge(sXW[sl2], 16 * ((4 * ss + 6 - sl2) // 3))
                        if ss >= 2:
                            sy.wait_ge(sPEx, ss - 1)
                        for c in range(NDC):
                            sy.dma_start(xnTb[sp][:, c * STR:(c + 1) * STR],
                                         xn_nat[ss * STR:(ss + 1) * STR, c * 128:(c + 1) * 128],
                                         transpose=True).then_inc(sTR[sp], 16)
                if True:
                    ss = NSTR - 1
                    sp = ss % 2
                    for sl2 in range(3):
                        sy.wait_ge(sXW[sl2], 16 * ((4 * ss + 6 - sl2) // 3))
                    sy.wait_ge(sPEx, ss - 1)
                    for c in range(NDC):
                        sy.dma_start(xnTb[sp][:, c * STR:(c + 1) * STR],
                                     xn_nat[ss * STR:(ss + 1) * STR, c * 128:(c + 1) * 128],
                                     transpose=True).then_inc(sTR[sp], 16)
                for sl2 in range(3):
                    sy.wait_ge(sXW[sl2], 16 * ((NTILE - sl2 + 2) // 3))
                pid = sy.partition_id()
                off = sy.compute_val(pid * OWN)
                sy.dma_start(xn_own_nat[:, :], xn_nat[bass.ds(off, OWN), :]).then_inc(sTOwn, 16)
                sy.wait_ge(sTOwn, 16)
                for c in range(NDC):
                    sy.dma_start(xnT_own[:, c * OWN:(c + 1) * OWN],
                                 xn_own_nat[:, c * 128:(c + 1) * 128],
                                 transpose=True).then_inc(sTOwn, 16)
                for pa in range(1, NP):
                    for ss in range(NSTR):
                        S2 = pa * NSTR + ss
                        sp = S2 % 2
                        sy.wait_ge(sPEx, S2 - 1)
                        for c in range(NDC):
                            sy.dma_start(xnTb[sp][:, c * STR:(c + 1) * STR],
                                         xn_nat[ss * STR:(ss + 1) * STR, c * 128:(c + 1) * 128],
                                         transpose=True).then_inc(sTR[sp], 16)

            @blkA.tensor
            def _(te):
                te.wait_ge(sINIT, 1)
                for pa in range(NP):
                    if pa >= 1:
                        te.wait_ge(sDVq, 16 * pa)
                    for s in range(NSTR):
                        S2 = pa * NSTR + s
                        sp = S2 % 2
                        te.wait_ge(sTR[sp], 256 * (S2 // 2 + 1))
                        for kb in range(HP):
                            if s == 0 and kb == 0:
                                te.wait_ge(sWk, 16 * (pa + 1))
                            if s == 0 and kb == 4:
                                te.wait_ge(sWkB, 16 * (pa + 1))
                            te.wait_ge(sDVkv, max(0, 16 * S2 + kb - 1))
                            for c in range(NDC):
                                ins = te.matmul(pC[kb % 2][:, :],
                                                wk_sb[:, c * 1024 + kb * 128: c * 1024 + (kb + 1) * 128],
                                                xnTb[sp][:, c * STR:(c + 1) * STR],
                                                start=(c == 0), stop=(c == NDC - 1))
                            ins.then_inc(sPEkvp, 1)
                        for tb in range(4):
                            for half in range(2):
                                hv = tb * 2 + half
                                if s == 0 and hv == 0:
                                    te.wait_ge(sWv, 16 * (pa + 1))
                                if s == 0 and hv == 1:
                                    te.wait_ge(sWvB, 16 * (pa + 1))
                                te.wait_ge(sDVkv, max(0, 16 * S2 + HP + hv - 1))
                                for c in range(NDC):
                                    ins = te.matmul(pC[2 + hv % 2][:, :],
                                                    xnTb[sp][:, c * STR + tb * 128: c * STR + (tb + 1) * 128],
                                                    wv_sb[:, c * 1024 + half * 512: c * 1024 + (half + 1) * 512],
                                                    start=(c == 0), stop=(c == NDC - 1))
                                if hv < HP - 1:
                                    ins.then_inc(sPEkvp, 1)
                                else:
                                    ins.then_inc(sPEx, 1)
                    if pa == 0:
                        te.wait_ge(sTOwn, 16 * (NDC + 1))
                    te.wait_ge(sDVkv, 16 * NSTR * (pa + 1))
                    for qb in range(16):
                        gq = 16 * pa + qb
                        hrel = qb // 2
                        ist = qb % 2
                        if qb == 0:
                            te.wait_ge(sWq, 16 * (pa + 1))
                        if qb == 8:
                            te.wait_ge(sWqB, 16 * (pa + 1))
                        if qb >= 2:
                            te.wait_ge(sDVq, gq - 1)
                        for c in range(NDC):
                            ins = te.matmul(pC[qb % 2][:, :],
                                            wk_sb[:, c * 1024 + hrel * 128: c * 1024 + (hrel + 1) * 128],
                                            xnT_own[:, c * OWN + ist * STR: c * OWN + (ist + 1) * STR],
                                            start=(c == 0), stop=(c == NDC - 1))
                        ins.then_inc(sPEqs, 1)

        with ExitStack() as sb1:
            kT_sb = [sb1.enter_context(nc.sbuf_tensor(f"kT{i}", [128, NT], BF16)) for i in range(2)]
            vb = [sb1.enter_context(nc.sbuf_tensor(f"vb{i}", [128, 64 * VSLOT], BF16)) for i in range(2)]
            PT = [sb1.enter_context(nc.sbuf_tensor(f"PT{i}", [128, 512], BF16)) for i in range(2)]
            ctx_sb = [sb1.enter_context(nc.sbuf_tensor(f"ctx{i}", [128, 1024], BF16)) for i in range(2)]
            craw = [sb1.enter_context(nc.sbuf_tensor(f"craw{i}", [128, 1024], F32)) for i in range(2)]
            rec_sb = sb1.enter_context(nc.sbuf_tensor("rec_sb", [128, 4], F32))
            wg_sb = [sb1.enter_context(nc.sbuf_tensor(f"wg{i}", [128, 2048], BF16)) for i in range(2)]
            wu_sb = [sb1.enter_context(nc.sbuf_tensor(f"wu{i}", [128, 2048], BF16)) for i in range(2)]
            esb = [sb1.enter_context(nc.sbuf_tensor(f"esb{i}", [128, 512], F32)) for i in range(2)]
            rsb = [sb1.enter_context(nc.sbuf_tensor(f"rsb{i}", [128, 512], F32)) for i in range(2)]
            gated_sb = [sb1.enter_context(nc.sbuf_tensor(f"gat{i}", [128, 1024], BF16)) for i in range(2)]

            Sbuf = [pS[0][:, 0:512], pS[0][:, 512:1024]]
            ctxbank = [(pC[0], 0), (pC[0], 256), (pC[1], 0), (pC[1], 256)]
            Gb = [pS[1][:, 0:512], pS[1][:, 512:1024]]
            Ub = [pC[2][:, :], pC[3][:, :]]

            blkB1 = sb1.enter_context(nc.Block())

            A_PE = None

            @blkB1.gpsimd
            def _(g):
                g.wait_ge(sDVq, 16 * NP)
                g.wait_ge(sKW[0], 16 * (8 * NSTR * NP))
                g.wait_ge(sKW[1], 16 * (8 * NSTR * NP))
                for sl2 in range(3):
                    g.wait_ge(sXW[sl2], 16 * ((NTILE - sl2 + 2) // 3))
                g.wait_ge(sTOwn, 16 * (NDC + 1))
                for h in range(NH):
                    hp = h % 2
                    if h >= 2:
                        g.wait_ge(sPEpv, 2 * (h - 1))
                    g.dma_start(kT_sb[hp][:, :], kT_scr[h, :, :]).then_inc(sKL[hp], 16)
                    vsrc = v_scr[:, h, :].rearrange("(jc p) c -> p jc c", p=128)
                    vdst = vb[hp][:, :].rearrange("p (jc w) -> p jc w", w=VSLOT)[:, :, 0:128]
                    for part in range(8):
                        g.dma_start(vdst[:, part * 8:(part + 1) * 8, :],
                                    vsrc[:, part * 8:(part + 1) * 8, :]).then_inc(sVL[hp], 16)
                    for b in range(4):
                        hb = 4 * h + b
                        bp = hb % 2
                        if hb >= 2:
                            g.wait_ge(sPEg, 2 * hb - 2)
                        g.dma_start(wg_sb[bp][:, :], wgb[hb, :, :]).then_inc(sWgu[bp], 16)
                        g.dma_start(wu_sb[bp][:, :], wub[hb, :, :]).then_inc(sWgu[bp], 16)

            @blkB1.vector
            def _(v):
                v.wait_ge(sPEqs, 16 * NP)
                for i in range(2):
                    ones_view = vb[i][:, :].rearrange("p (s w) -> p s w", w=VSLOT)[:, :, 128:129]
                    ins = v.memset(ones_view, 1.0)
                ins.then_inc(sINIT, 1)
                for h in range(NH):
                    for ist in range(2):
                        u = h * 2 + ist
                        cp2 = u % 2

                        def gate_mult(seg):
                            gsb = u * 4 + seg
                            sp2 = gsb % 2
                            hb = gsb // 2
                            half = gsb % 2
                            v.wait_ge(sSil, gsb + 1)
                            if half == 0 and hb >= 2:
                                v.wait_ge(sGW[hb % 2], 16 * (hb // 2))
                            v.tensor_scalar_add(rsb[sp2][:, :], esb[sp2][:, :], cst[:, 5:6])
                            v.drain()
                            v.reciprocal(esb[sp2][:, :], rsb[sp2][:, :])
                            v.drain()
                            v.tensor_tensor(rsb[sp2][:, :], Gb[sp2], esb[sp2][:, :], ALU.mult)
                            v.drain()
                            v.tensor_tensor(gated_sb[hb % 2][:, half * 512:(half + 1) * 512],
                                            rsb[sp2][:, :], Ub[sp2],
                                            ALU.mult).then_inc(sDVgu, 1)
                            v.drain()

                        for seg in range(3):
                            gate_mult(seg)
                        v.wait_ge(sPEpv, u + 1)
                        v.drain()
                        v.tensor_copy(craw[cp2][:, 0:512], pC[0][:, :]).then_inc(sDVctx, 1)
                        v.tensor_copy(craw[cp2][:, 512:1024], pC[1][:, :]).then_inc(sDVctx, 1)
                        v.drain()
                        gate_mult(3)
                        if h >= 2 and ist == 0:
                            v.wait_ge(sCTXW[h % 2], 16 * (h // 2))
                        for ib in range(4):
                            base = (ib // 2) * 512 + (ib % 2) * 256
                            v.reciprocal(rec_sb[:, 0:1], craw[cp2][:, base + 128:base + 129])
                            v.drain()
                            col = ist * 512 + ib * 128
                            v.tensor_scalar_mul(ctx_sb[h % 2][:, col:col + 128],
                                                craw[cp2][:, base:base + 128],
                                                rec_sb[:, 0:1]).then_inc(sDVN, 1)
                            v.drain()

            @blkB1.scalar
            def _(sc):
                sc.wait_ge(sINIT, 2)
                for h in range(NH):
                    for ist in range(2):
                        u = h * 2 + ist
                        for jc in range(64):
                            gc = u * 64 + jc
                            pp = jc % 2
                            sc.wait_ge(sS, gc + 1)
                            if gc >= 2:
                                if jc == 0:
                                    sc.wait_ge(sPVd, u * 63)
                                elif jc == 1:
                                    sc.wait_ge(sPEpv, u)
                                else:
                                    sc.wait_ge(sPVd, u * 63 + jc - 1)
                            sc.activation(PT[pp][:, :], Sbuf[pp], AF.Exp,
                                          scale=cst[:, 1:2], bias=cst[:, 2:3]).then_inc(sEXP[pp], 1)
                            if jc % 16 == 15:
                                gsb = u * 4 + jc // 16
                                sp2 = gsb % 2
                                sc.wait_ge(sPEg, gsb + 1)
                                if gsb >= 2:
                                    sc.wait_ge(sDVgu, gsb - 1)
                                sc.activation(esb[sp2][:, :], Gb[sp2], AF.Exp,
                                              scale=cst[:, 4:5]).then_inc(sSil, 1)

            @blkB1.sync
            def _(sy):
                for h in range(NH):
                    for b in range(4):
                        hb = 4 * h + b
                        sy.wait_ge(sDVgu, 2 * hb + 2)
                        if hb >= 2:
                            sy.wait_ge(sGW[hb % 2], 16 * (hb // 2))
                        sy.dma_start(gated_scr[hb, :, :],
                                     gated_sb[hb % 2][:, :]).then_inc(sGW[hb % 2], 16)
                    sy.wait_ge(sDVN, 4 * (2 * h + 2))
                    if h >= 2:
                        sy.wait_ge(sCTXW[h % 2], 16 * (h // 2))
                    src = ctx_sb[h % 2][:, :].rearrange("p (b c) -> p b c", c=128)
                    dst = ctx_nat[:, :].rearrange("(b p) (hh c) -> hh p b c", p=128, c=128)[h]
                    sy.dma_start(dst, src).then_inc(sCTXW[h % 2], 16)

            @blkB1.tensor
            def _(te):
                te.wait_ge(sINIT, 2)
                for h in range(NH):
                    hp = h % 2
                    te.wait_ge(sKL[hp], 16 * (h // 2 + 1))
                    te.wait_ge(sVL[hp], 128 * (h // 2 + 1))
                    for ist in range(2):
                        u = h * 2 + ist
                        qslice = qT[:, h * OWN + ist * STR: h * OWN + (ist + 1) * STR]

                        def s_mm(jc):
                            gc = u * 64 + jc
                            pp = jc % 2
                            if gc >= 2:
                                te.wait_ge(sEXP[pp], gc // 2)
                            te.matmul(Sbuf[pp],
                                      kT_sb[hp][:, jc * 128:(jc + 1) * 128],
                                      qslice, start=True, stop=True).then_inc(sS, 1)

                        s_mm(0)
                        for jc in range(64):
                            gc = u * 64 + jc
                            pp = jc % 2
                            seg = jc // 16
                            gsb = u * 4 + seg
                            sp2 = gsb % 2
                            hb = gsb // 2
                            half = gsb % 2
                            bp = hb % 2
                            if jc < 63:
                                s_mm(jc + 1)
                            if jc % 16 == 0:
                                te.wait_ge(sWgu[bp], 32 * (hb // 2 + 1))
                                if gsb >= 2:
                                    te.wait_ge(sSil, gsb - 1)
                                    te.wait_ge(sDVgu, gsb - 1)
                            for m in (2 * (jc % 16), 2 * (jc % 16) + 1):
                                if m < 16:
                                    te.matmul(Gb[sp2],
                                              wg_sb[bp][:, m * 128:(m + 1) * 128],
                                              xnT_own[:, m * OWN + half * 512: m * OWN + (half + 1) * 512],
                                              start=(m == 0), stop=(m == 15))
                                else:
                                    c = m - 16
                                    ins_g = te.matmul(Ub[sp2],
                                                      wu_sb[bp][:, c * 128:(c + 1) * 128],
                                                      xnT_own[:, c * OWN + half * 512: c * OWN + (half + 1) * 512],
                                                      start=(c == 0), stop=(c == 15))
                                    if c == 15:
                                        ins_g.then_inc(sPEg, 1)
                            if jc == 0:
                                te.wait_ge(sDVctx, 2 * u)
                            te.wait_ge(sEXP[pp], gc // 2 + 1)
                            for ib in range(4):
                                bank, boff = ctxbank[ib]
                                ins = te.matmul(bank[:, boff:boff + 129],
                                                PT[pp][:, ib * 128:(ib + 1) * 128],
                                                vb[hp][:, jc * VSLOT: jc * VSLOT + 129],
                                                start=(jc == 0 and boff == 0),
                                                stop=(jc == 63 and boff == 256))
                            if jc < 63:
                                ins.then_inc(sPVd, 1)
                            else:
                                ins.then_inc(sPEpv, 1)

        with ExitStack() as sb2:
            gatedin = [sb2.enter_context(nc.sbuf_tensor(f"gin{i}", [128, 4096], BF16)) for i in range(3)]
            wd_sb = [sb2.enter_context(nc.sbuf_tensor(f"wd{i}", [128, 2048], BF16)) for i in range(3)]
            wpq_sb = [sb2.enter_context(nc.sbuf_tensor(f"wpq{i}", [128, 8192], BF16)) for i in range(2)]
            ctxT_sb = sb2.enter_context(nc.sbuf_tensor("ctxT_sb", [128, 16 * OWN], BF16,
                                                       is_transpose_dest=True))
            xo_sb = [sb2.enter_context(nc.sbuf_tensor(f"xo{i}", [128, 512], F32)) for i in range(4)]
            out_sb = [sb2.enter_context(nc.sbuf_tensor(f"os{i}", [128, 512], F32)) for i in range(2)]
            bias_sb = sb2.enter_context(nc.sbuf_tensor("bias_sb", [128, D], F32))

            banks = [pS[0][:, 0:512], pS[0][:, 512:1024],
                     pS[1][:, 0:512], pS[1][:, 512:1024],
                     pC[0][:, :], pC[1][:, :], pC[2][:, :], pC[3][:, :]]

            blkB2 = sb2.enter_context(nc.Block())

            def b2_barrier(eng):
                eng.wait_ge(sPEg, 128)
                eng.wait_ge(sSil, 128)
                eng.wait_ge(sDVgu, 128)
                eng.wait_ge(sDVctx, 64)
                eng.wait_ge(sDVN, 128)
                eng.wait_ge(sEXP[0], 1024)
                eng.wait_ge(sEXP[1], 1024)
                eng.wait_ge(sCTXW[0], 16 * 8)
                eng.wait_ge(sCTXW[1], 16 * 8)
                eng.wait_ge(sGW[0], 16 * 32)
                eng.wait_ge(sGW[1], 16 * 32)

            @blkB2.gpsimd
            def _(g):
                b2_barrier(g)
                g.dma_start(bias_sb[:, :], biasb[:, :]).then_inc(sBB, 16)
                for q in range(4):
                    if q >= 2:
                        g.wait_ge(sPEob, 8 * (q - 1))
                    g.dma_start(wpq_sb[q % 2][:, :], wpq[q, :, :]).then_inc(sWp[q % 2], 16)
                    for grp in range(16):
                        n = 16 * q + grp
                        cp = n % 3
                        if n >= 3:
                            g.wait_ge(sPEhc, n - 2)
                        g.dma_start(wd_sb[cp][:, :].rearrange("p (s f) -> p s f", s=4),
                                    wdq[q, 4 * grp:4 * grp + 4, :, :].rearrange(
                                        "s p f -> p s f")).then_inc(sGin[cp], 16)

            @blkB2.sync
            def _(sy):
                b2_barrier(sy)
                for hc in range(NDC):
                    sy.dma_start(ctxT_sb[:, hc * OWN:(hc + 1) * OWN],
                                 ctx_nat[:, hc * 128:(hc + 1) * 128],
                                 transpose=True).then_inc(sTctx, 16)
                for q in range(4):
                    for grp in range(16):
                        n = 16 * q + grp
                        cp = n % 3
                        if n >= 3:
                            sy.wait_ge(sPEhc, n - 2)
                        sy.dma_start(gatedin[cp][:, :].rearrange("p (s f) -> p s f", s=4),
                                     gated_scr[4 * grp:4 * grp + 4, :, :].rearrange(
                                         "s p f -> p s f")).then_inc(sGR[cp], 16)
            @blkB2.scalar
            def _(sc):
                b2_barrier(sc)

                def xo_load(n):
                    q, tb = n // 8, n % 8
                    sc.dma_start(xo_sb[n % 4][:, :],
                                 x_own[tb * 128:(tb + 1) * 128, q * 512:(q + 1) * 512]
                                 ).then_inc(sXO[n % 4], 16)

                for n0 in range(4):
                    xo_load(n0)
                for n in range(32):
                    q, tb = n // 8, n % 8
                    sc.wait_ge(sDVo, n + 1)
                    if n >= 2:
                        sc.wait_ge(sOUT[tb % 2], 16 * (n // 2))
                    sc.dma_start(out[tb * 128:(tb + 1) * 128, q * 512:(q + 1) * 512],
                                 out_sb[tb % 2][:, :]).then_inc(sOUT[tb % 2], 16)
                    if n + 4 < 32:
                        xo_load(n + 4)
                sc.wait_ge(sOUT[0], 16 * 16)
                sc.wait_ge(sOUT[1], 16 * 16)

            @blkB2.tensor
            def _(te):
                for q in range(4):
                    for grp in range(16):
                        n = 16 * q + grp
                        cp = n % 3
                        te.wait_ge(sGin[cp], 16 * (n // 3 + 1))
                        te.wait_ge(sGR[cp], 16 * (n // 3 + 1))
                        for sub in range(4):
                            for tb in range(8):
                                if grp == 0 and sub == 0 and q >= 1:
                                    te.wait_ge(sDVo, 8 * (q - 1) + tb + 1)
                                ins = te.matmul(banks[tb],
                                                gatedin[cp][:, sub * 1024 + tb * 128: sub * 1024 + (tb + 1) * 128],
                                                wd_sb[cp][:, sub * 512:(sub + 1) * 512],
                                                start=(grp == 0 and sub == 0), stop=False)
                        ins.then_inc(sPEhc, 1)
                    te.wait_ge(sWp[q % 2], 16 * (q // 2 + 1))
                    if q == 0:
                        te.wait_ge(sTctx, 16 * NDC)
                    for hc2 in range(NDC):
                        for tb in range(8):
                            ins = te.matmul(banks[tb],
                                            ctxT_sb[:, hc2 * OWN + tb * 128: hc2 * OWN + (tb + 1) * 128],
                                            wpq_sb[q % 2][:, hc2 * 512:(hc2 + 1) * 512],
                                            start=False, stop=(hc2 == NDC - 1))
                            if hc2 == NDC - 1:
                                ins.then_inc(sPEob, 1)

            @blkB2.vector
            def _(v):
                v.wait_ge(sBB, 16)

                def tt1(n):
                    q, tb = n // 8, n % 8
                    tp = tb % 2
                    v.wait_ge(sXO[n % 4], 16 * (n // 4 + 1))
                    if n >= 2:
                        v.wait_ge(sOUT[tp], 16 * (n // 2))
                    v.tensor_tensor(out_sb[tp][:, :], xo_sb[n % 4][:, :],
                                    bias_sb[:, q * 512:(q + 1) * 512], ALU.add)
                    v.drain()

                tt1(0)
                tt1(1)
                for n in range(32):
                    q, tb = n // 8, n % 8
                    tp = tb % 2
                    v.wait_ge(sPEob, n + 1)
                    v.tensor_tensor(out_sb[tp][:, :], out_sb[tp][:, :],
                                    banks[tb], ALU.add).then_inc(sDVo, 1)
                    v.drain()
                    if n + 2 < 32:
                        tt1(n + 2)

    return nc


def kernel(L0_attn_query, L0_attn_key, L0_attn_value, L0_attn_Wqkv, L0_attn_Wproj,
           L0_attn_bproj, L0_ffn_w_gate, L0_ffn_w_up, L0_ffn_w_down):
    x = np.ascontiguousarray(np.asarray(L0_attn_query, dtype=np.float32).reshape(NT, D))
    Wqkv = np.asarray(L0_attn_Wqkv, dtype=np.float32)
    Wproj = np.asarray(L0_attn_Wproj, dtype=np.float32)
    bproj = np.asarray(L0_attn_bproj, dtype=np.float32)
    w_gate = np.asarray(L0_ffn_w_gate, dtype=np.float32)
    w_up = np.asarray(L0_ffn_w_up, dtype=np.float32)
    w_down = np.asarray(L0_ffn_w_down, dtype=np.float32)

    Wr = Wqkv.reshape(NH, 3, 128, D)
    wq = Wr[:, 0].reshape(NH * 128, D)
    wk = Wr[:, 1].reshape(NH * 128, D)
    wv = Wr[:, 2].reshape(NH * 128, D)

    def per_pass(w):
        return np.ascontiguousarray(
            w.T.reshape(16, 128, NP, HP * 128).transpose(2, 1, 0, 3)).reshape(NP, 128, HP * 2048)

    ins = dict(
        x=x,
        wqp=per_pass(wq), wkp=per_pass(wk), wvp=per_pass(wv),
        wgb=np.ascontiguousarray(
            w_gate.T.reshape(16, 128, 64, 128).transpose(2, 1, 0, 3)).reshape(64, 128, 2048),
        wub=np.ascontiguousarray(
            w_up.T.reshape(16, 128, 64, 128).transpose(2, 1, 0, 3)).reshape(64, 128, 2048),
        wdq=np.ascontiguousarray(w_down.T.reshape(64, 128, 4, 512).transpose(2, 0, 1, 3)),
        wpq=np.ascontiguousarray(
            Wproj.T.reshape(16, 128, 4, 512).transpose(2, 1, 0, 3)).reshape(4, 128, 8192),
        biasb=np.ascontiguousarray(np.broadcast_to(bproj[None, :], (128, D))),
    )

    nc = build_nc()
    in_maps = []
    for c in range(8):
        m = dict(ins)
        m["x_own"] = np.ascontiguousarray(x[c * OWN:(c + 1) * OWN])
        in_maps.append(m)
    res = run_bass_kernel_spmd(nc, in_maps, core_ids=list(range(8)))
    full = np.concatenate([res.results[c]["out"] for c in range(8)], axis=0)
    return full.reshape(1, NT, D)


# version 16
# speedup vs baseline: 1.1019x; 1.1019x over previous
import numpy as np
from contextlib import ExitStack

import concourse.bass as bass
import concourse.mybir as mybir
from concourse.bass_utils import run_bass_kernel_spmd

F32 = mybir.dt.float32
BF16 = mybir.dt.bfloat16
AF = mybir.ActivationFunctionType
AX = mybir.AxisListType
ALU = mybir.AluOpType

D = 2048
NT = 8192
OWN = 1024
NH = 16
NP = 2
HP = 8
NDC = 16
NSTR = 16
STR = 512
NTILE = 64
NHB = 64
VSLOT = 132

INVD = 1.0 / 2048.0
EPS = 1e-5
QKSCALE = 0.08838834764831845
CSHIFT = 4.0


def build_nc():
    nc = bass.Bass("TRN2", target_bir_lowering=False)

    x = nc.dram_tensor("x", [NT, D], F32, kind="ExternalInput")
    x_own = nc.dram_tensor("x_own", [OWN, D], F32, kind="ExternalInput")
    wqp = nc.dram_tensor("wqp", [NP, 128, HP * 2048], F32, kind="ExternalInput")
    wkp = nc.dram_tensor("wkp", [NP, 128, HP * 2048], F32, kind="ExternalInput")
    wvp = nc.dram_tensor("wvp", [NP, 128, HP * 2048], F32, kind="ExternalInput")
    wgb = nc.dram_tensor("wgb", [NHB, 128, 2048], F32, kind="ExternalInput")
    wub = nc.dram_tensor("wub", [NHB, 128, 2048], F32, kind="ExternalInput")
    wdq = nc.dram_tensor("wdq", [4, 64, 128, 512], F32, kind="ExternalInput")
    wpq = nc.dram_tensor("wpq", [4, 128, 8192], F32, kind="ExternalInput")
    biasb = nc.dram_tensor("biasb", [128, D], F32, kind="ExternalInput")
    out = nc.dram_tensor("out", [OWN, D], F32, kind="ExternalOutput")

    xn_nat = nc.dram_tensor("xn_nat", [NT, D], BF16, kind="Internal")
    wkb = nc.dram_tensor("wkb", [NP, 128, HP * 2048], BF16, kind="Internal")
    wvb = nc.dram_tensor("wvb", [NP, 128, HP * 2048], BF16, kind="Internal")
    wqb = nc.dram_tensor("wqb", [NP, 128, HP * 2048], BF16, kind="Internal")
    wdb = nc.dram_tensor("wdb", [4, 64, 128, 512], BF16, kind="Internal")
    wpb = nc.dram_tensor("wpb", [4, 128, 8192], BF16, kind="Internal")
    xn_own_nat = nc.dram_tensor("xn_own_nat", [OWN, D], BF16, kind="Internal")
    kT_scr = nc.dram_tensor("kT_scr", [NH, 128, NT], BF16, kind="Internal")
    v_scr = nc.dram_tensor("v_scr", [NT, NH, 128], BF16, kind="Internal")
    ctx_nat = nc.dram_tensor("ctx_nat", [OWN, D], BF16, kind="Internal")
    gated_scr = nc.dram_tensor("gated_scr", [NHB, 128, OWN], BF16, kind="Internal")

    st = ExitStack()
    with st:
        xnT_own = st.enter_context(nc.sbuf_tensor("xnT_own", [128, NDC * OWN], BF16))
        qT = st.enter_context(nc.sbuf_tensor("qT", [128, NH * OWN], BF16))
        cst = st.enter_context(nc.sbuf_tensor("cst", [128, 8], F32))
        pS = [st.enter_context(nc.psum_tensor(f"pS{i}", [128, 1024], F32)) for i in range(2)]
        pC = [st.enter_context(nc.psum_tensor(f"pC{i}", [128, 512], F32)) for i in range(4)]

        def sem(name):
            return st.enter_context(nc.semaphore(name))

        sXin = [sem("sXin0"), sem("sXin1")]
        sSq = sem("sSq")
        sVar = sem("sVar")
        sStd = sem("sStd")
        sNR = sem("sNR")
        sLN = sem("sLN")
        sXW = [sem(f"sXW{i}") for i in range(3)]
        sTOwn = sem("sTOwn")
        sTR = [sem("sTR0"), sem("sTR1")]
        sPEx = sem("sPEx")
        sWk = sem("sWk")
        sWv = sem("sWv")
        sWq = sem("sWq")
        sWkB = sem("sWkB")
        sWvB = sem("sWvB")
        sWqB = sem("sWqB")
        sPEkvp = sem("sPEkvp")
        sDVkv = sem("sDVkv")
        sKW = [sem("sKW0"), sem("sKW1")]
        sPEqs = sem("sPEqs")
        sDVq = sem("sDVq")
        sKL = [sem("sKL0"), sem("sKL1")]
        sVL = [sem("sVL0"), sem("sVL1")]
        sS = sem("sS")
        sEXP = [sem("sEXP0"), sem("sEXP1")]
        sPVd = sem("sPVd")
        sPEpv = sem("sPEpv")
        sDVctx = sem("sDVctx")
        sDVN = sem("sDVN")
        sCTXW = [sem("sCTXW0"), sem("sCTXW1")]
        sWgu = [sem("sWgu0"), sem("sWgu1")]
        sPEg = sem("sPEg")
        sSil = sem("sSil")
        sDVgu = sem("sDVgu")
        sGW = [sem("sGW0"), sem("sGW1")]
        sTctx = sem("sTctx")
        sTctx2 = sem("sTctx2")
        sWp = [sem("sWp0"), sem("sWp1")]
        sGin = [sem(f"sGin{i}") for i in range(3)]
        sGR = [sem(f"sGR{i}") for i in range(3)]
        sPEhc = sem("sPEhc")
        sPEob = sem("sPEob")
        sXO = [sem(f"sXO{i}") for i in range(4)]
        sDVo = sem("sDVo")
        sOUT = [sem("sOUT0"), sem("sOUT1")]
        sBB = sem("sBB")
        sINIT = sem("sINIT")
        sWC = sem("sWC")

        with ExitStack() as sa:
            xinp = [sa.enter_context(nc.sbuf_tensor(f"xinp{i}", [128, 2 * D], BF16)) for i in range(2)]
            stats = [sa.enter_context(nc.sbuf_tensor(f"stats{i}", [128, 16], F32)) for i in range(4)]
            xn_bf = [sa.enter_context(nc.sbuf_tensor(f"xnbf{i}", [128, D], BF16)) for i in range(3)]
            xnTb = [sa.enter_context(nc.sbuf_tensor(f"xnTb{i}", [128, NDC * STR], BF16,
                                                    is_transpose_dest=True)) for i in range(2)]
            wk_sb = sa.enter_context(nc.sbuf_tensor("wk_sb", [128, HP * 2048], BF16))
            wv_sb = sa.enter_context(nc.sbuf_tensor("wv_sb", [128, HP * 2048], BF16))
            kst = [sa.enter_context(nc.sbuf_tensor(f"kst{i}", [128, 512], BF16)) for i in range(2)]
            vst = kst

            blkA = sa.enter_context(nc.Block())

            def whalf(ap, hf):
                return ap.rearrange("p (c w) -> p c w", w=1024)[:, :, hf * 512:(hf + 1) * 512]

            @blkA.gpsimd
            def _(g):
                def x_pair(j):
                    jp = j % 2
                    if j >= 2:
                        g.wait_ge(sLN, 2 * j - 2)
                    g.dma_start(xinp[jp][:, :].rearrange("p (s f) -> p s f", s=2),
                                x[j * 256:(j + 1) * 256, :].rearrange("(s p) f -> p s f", p=128)
                                ).then_inc(sXin[jp], 16)

                x_pair(0)
                x_pair(1)
                g.dma_start(whalf(wk_sb[:, :], 0), whalf(wkp[0], 0)).then_inc(sWk, 16)
                x_pair(2)
                g.dma_start(whalf(wk_sb[:, :], 1), whalf(wkp[0], 1)).then_inc(sWkB, 16)
                x_pair(3)
                g.dma_start(whalf(wv_sb[:, :], 0), whalf(wvp[0], 0)).then_inc(sWv, 16)
                x_pair(4)
                g.dma_start(whalf(wv_sb[:, :], 1), whalf(wvp[0], 1)).then_inc(sWvB, 16)
                for j in range(5, NTILE // 2):
                    x_pair(j)
                g.dma_start(wqb[0, :, :], wqp[0, :, :]).then_inc(sWC, 16)
                for pa in range(1, NP):
                    g.dma_start(wkb[pa, :, :], wkp[pa, :, :]).then_inc(sWC, 16)
                    g.dma_start(wvb[pa, :, :], wvp[pa, :, :]).then_inc(sWC, 16)
                    g.dma_start(wqb[pa, :, :], wqp[pa, :, :]).then_inc(sWC, 16)
                g.wait_ge(sWC, 16 * (3 * NP - 2))
                for pa in range(NP):
                    g.wait_ge(sPEkvp, 15 * (NSTR * (pa + 1) - 1) + 8)
                    g.dma_start(whalf(wk_sb[:, :], 0), whalf(wqb[pa], 0)).then_inc(sWq, 16)
                    g.dma_start(whalf(wk_sb[:, :], 1), whalf(wqb[pa], 1)).then_inc(sWqB, 16)
                    if pa + 1 < NP:
                        g.wait_ge(sPEx, NSTR * (pa + 1))
                        g.dma_start(whalf(wv_sb[:, :], 0), whalf(wvb[pa + 1], 0)).then_inc(sWv, 16)
                        g.wait_ge(sPEqs, 16 * pa + 8)
                        g.dma_start(whalf(wk_sb[:, :], 0), whalf(wkb[pa + 1], 0)).then_inc(sWk, 16)
                        g.dma_start(whalf(wv_sb[:, :], 1), whalf(wvb[pa + 1], 1)).then_inc(sWvB, 16)
                        g.wait_ge(sPEqs, 16 * (pa + 1))
                        g.dma_start(whalf(wk_sb[:, :], 1), whalf(wkb[pa + 1], 1)).then_inc(sWkB, 16)
                        if pa == NP - 2:
                            for qq in range(4):
                                g.dma_start(wdb[qq, :, :, :], wdq[qq, :, :, :]).then_inc(sWC, 16)
                            for qq in range(4):
                                g.dma_start(wpb[qq, :, :], wpq[qq, :, :]).then_inc(sWC, 16)

            @blkA.vector
            def _(v):
                v.memset(cst[:, 0:1], EPS)
                v.memset(cst[:, 1:2], QKSCALE)
                v.memset(cst[:, 2:3], -CSHIFT)
                v.memset(cst[:, 3:4], INVD)
                v.memset(cst[:, 4:5], -1.0)
                v.memset(cst[:, 5:6], 1.0).then_inc(sINIT, 1)

                def ln_tile(t):
                    sl = t % 4
                    S = stats[sl]
                    xv = xinp[(t // 2) % 2][:, (t % 2) * D:(t % 2 + 1) * D]
                    v.wait_ge(sXin[(t // 2) % 2], 16 * (t // 4 + 1))
                    v.reduce_sum(S[:, 0:1], xv, axis=AX.X)
                    v.drain()
                    v.wait_ge(sSq, t + 1)
                    v.tensor_scalar_mul(S[:, 2:3], S[:, 0:1], cst[:, 3:4])
                    v.drain()
                    v.tensor_mul(S[:, 3:4], S[:, 2:3], S[:, 2:3])
                    v.drain()
                    v.scalar_tensor_tensor(S[:, 4:5], S[:, 1:2], cst[:, 3:4], S[:, 3:4],
                                           ALU.mult, ALU.subtract).then_inc(sVar, 1)
                    v.wait_ge(sStd, t + 1)
                    v.reciprocal(S[:, 6:7], S[:, 5:6])
                    v.drain()
                    v.scalar_tensor_tensor(S[:, 7:8], S[:, 2:3], cst[:, 4:5], S[:, 6:7],
                                           ALU.mult, ALU.mult).then_inc(sNR, 1)
                    v.drain()
                    v.tensor_scalar(xn_bf[t % 3][:, :], xv, S[:, 6:7], S[:, 7:8],
                                    ALU.mult, ALU.add).then_inc(sLN, 1)

                LN_SLOTS = {1: 0, 2: 1, 5: 2, 8: 3}

                def kv_evac_stripe(pa, s, lns=()):
                    S2 = pa * NSTR + s
                    for i in range(2 * HP):
                        if i in LN_SLOTS and LN_SLOTS[i] < len(lns):
                            ln_tile(lns[LN_SLOTS[i]])
                        n = 16 * S2 + i
                        if i < HP:
                            kb = i
                            v.wait_ge(sPEkvp, 15 * S2 + kb + 1)
                            if n >= 2:
                                v.wait_ge(sKW[n % 2], 16 * (n // 2))
                            v.tensor_copy(kst[n % 2][:, :], pC[kb % 4][:, :]).then_inc(sDVkv, 1)
                        else:
                            hv = i - HP
                            if hv < HP - 1:
                                v.wait_ge(sPEkvp, 15 * S2 + HP + hv + 1)
                            else:
                                v.wait_ge(sPEx, S2 + 1)
                            v.wait_ge(sKW[n % 2], 16 * (n // 2))
                            v.tensor_copy(vst[n % 2][:, :], pC[2 + hv % 2][:, :]).then_inc(sDVkv, 1)

                def q_evac(pa):
                    for qb in range(16):
                        gq = 16 * pa + qb
                        v.wait_ge(sPEqs, gq + 1)
                        h = pa * HP + qb // 2
                        ist = qb % 2
                        v.tensor_copy(qT[:, h * OWN + ist * STR: h * OWN + (ist + 1) * STR],
                                      pC[qb % 2][:, :]).then_inc(sDVq, 1)

                LOOK = 12
                for tt in range(LOOK):
                    ln_tile(tt)
                for s in range(NSTR):
                    kv_evac_stripe(0, s,
                                   list(range(4 * s + LOOK, min(4 * s + LOOK + 4, NTILE))))
                q_evac(0)
                for pa in range(1, NP):
                    for s in range(NSTR):
                        kv_evac_stripe(pa, s)
                    q_evac(pa)

            @blkA.scalar
            def _(sc):
                sc.wait_ge(sINIT, 1)

                def ln_acts(t):
                    sl = t % 4
                    S = stats[sl]
                    xv = xinp[(t // 2) % 2][:, (t % 2) * D:(t % 2 + 1) * D]
                    sc.wait_ge(sXin[(t // 2) % 2], 16 * (t // 4 + 1))
                    if t >= 4:
                        sc.wait_ge(sLN, t - 3)
                    if t >= 3:
                        sc.wait_ge(sLN, t - 2)
                        sc.wait_ge(sXW[t % 3], 16 * (t // 3))
                    sc.drain()
                    sc.activation(xn_bf[t % 3][:, :], xv, AF.Square,
                                  accum_out=S[:, 1:2]).then_inc(sSq, 1)
                    sc.wait_ge(sVar, t + 1)
                    sc.activation(S[:, 5:6], S[:, 4:5], AF.Sqrt,
                                  bias=cst[:, 0:1]).then_inc(sStd, 1)

                LN_SLOTS = {1: 0, 2: 1, 5: 2, 8: 3}

                def kv_write_stripe(pa, s, lns=()):
                    S2 = pa * NSTR + s
                    for i in range(2 * HP):
                        if i in LN_SLOTS and LN_SLOTS[i] < len(lns):
                            ln_acts(lns[LN_SLOTS[i]])
                        n = 16 * S2 + i
                        sc.wait_ge(sDVkv, n + 1)
                        if n >= 2:
                            sc.wait_ge(sKW[n % 2], 16 * (n // 2))
                        if i < HP:
                            h = pa * HP + i
                            sc.dma_start(kT_scr[h, :, s * STR:(s + 1) * STR],
                                         kst[n % 2][:, :]).then_inc(sKW[n % 2], 16)
                        else:
                            hv = i - HP
                            tb, half = hv // 2, hv % 2
                            dst = v_scr[s * STR + tb * 128: s * STR + (tb + 1) * 128,
                                        pa * HP + half * 4: pa * HP + (half + 1) * 4, :]
                            src = vst[n % 2][:, :].rearrange("p (h c) -> p h c", c=128)
                            sc.dma_start(dst, src).then_inc(sKW[n % 2], 16)

                LOOK = 12
                for tt in range(LOOK):
                    ln_acts(tt)
                for s in range(NSTR):
                    kv_write_stripe(0, s,
                                    list(range(4 * s + LOOK, min(4 * s + LOOK + 4, NTILE))))
                for pa in range(1, NP):
                    for s in range(NSTR):
                        kv_write_stripe(pa, s)

            @blkA.sync
            def _(sy):
                for t in range(NTILE):
                    sl = t % 3
                    sy.wait_ge(sLN, t + 1)
                    if t >= 3:
                        sy.wait_ge(sXW[sl], 16 * (t // 3))
                    sy.dma_start(xn_nat[t * 128:(t + 1) * 128, :], xn_bf[sl][:, :]).then_inc(sXW[sl], 16)
                    if t % 4 == 3 and t >= 4:
                        ss = t // 4 - 1
                        sp = ss % 2
                        for sl2 in range(3):
                            sy.wait_ge(sXW[sl2], 16 * ((4 * ss + 6 - sl2) // 3))
                        if ss >= 2:
                            sy.wait_ge(sPEx, ss - 1)
                        for c in range(NDC):
                            sy.dma_start(xnTb[sp][:, c * STR:(c + 1) * STR],
                                         xn_nat[ss * STR:(ss + 1) * STR, c * 128:(c + 1) * 128],
                                         transpose=True).then_inc(sTR[sp], 16)
                if True:
                    ss = NSTR - 1
                    sp = ss % 2
                    for sl2 in range(3):
                        sy.wait_ge(sXW[sl2], 16 * ((4 * ss + 6 - sl2) // 3))
                    sy.wait_ge(sPEx, ss - 1)
                    for c in range(NDC):
                        sy.dma_start(xnTb[sp][:, c * STR:(c + 1) * STR],
                                     xn_nat[ss * STR:(ss + 1) * STR, c * 128:(c + 1) * 128],
                                     transpose=True).then_inc(sTR[sp], 16)
                for sl2 in range(3):
                    sy.wait_ge(sXW[sl2], 16 * ((NTILE - sl2 + 2) // 3))
                pid = sy.partition_id()
                off = sy.compute_val(pid * OWN)
                sy.dma_start(xn_own_nat[:, :], xn_nat[bass.ds(off, OWN), :]).then_inc(sTOwn, 16)
                sy.wait_ge(sTOwn, 16)
                for c in range(NDC):
                    sy.dma_start(xnT_own[:, c * OWN:(c + 1) * OWN],
                                 xn_own_nat[:, c * 128:(c + 1) * 128],
                                 transpose=True).then_inc(sTOwn, 16)
                for pa in range(1, NP):
                    for ss in range(NSTR):
                        S2 = pa * NSTR + ss
                        sp = S2 % 2
                        sy.wait_ge(sPEx, S2 - 1)
                        for c in range(NDC):
                            sy.dma_start(xnTb[sp][:, c * STR:(c + 1) * STR],
                                         xn_nat[ss * STR:(ss + 1) * STR, c * 128:(c + 1) * 128],
                                         transpose=True).then_inc(sTR[sp], 16)

            @blkA.tensor
            def _(te):
                te.wait_ge(sINIT, 1)
                for pa in range(NP):
                    if pa >= 1:
                        te.wait_ge(sDVq, 16 * pa)
                    for s in range(NSTR):
                        S2 = pa * NSTR + s
                        sp = S2 % 2
                        te.wait_ge(sTR[sp], 256 * (S2 // 2 + 1))
                        for kb in range(HP):
                            if s == 0 and kb == 0:
                                te.wait_ge(sWk, 16 * (pa + 1))
                            if s == 0 and kb == 4:
                                te.wait_ge(sWkB, 16 * (pa + 1))
                            if kb < 2:
                                w = 16 * S2 - 11 + kb
                            elif kb < 4:
                                w = 16 * S2 - 3 + kb
                            else:
                                w = 16 * S2 + kb - 3
                            te.wait_ge(sDVkv, max(0, w))
                            for c in range(NDC):
                                ins = te.matmul(pC[kb % 4][:, :],
                                                wk_sb[:, c * 1024 + kb * 128: c * 1024 + (kb + 1) * 128],
                                                xnTb[sp][:, c * STR:(c + 1) * STR],
                                                start=(c == 0), stop=(c == NDC - 1))
                            ins.then_inc(sPEkvp, 1)
                        for tb in range(4):
                            for half in range(2):
                                hv = tb * 2 + half
                                if s == 0 and hv == 0:
                                    te.wait_ge(sWv, 16 * (pa + 1))
                                if s == 0 and hv == 1:
                                    te.wait_ge(sWvB, 16 * (pa + 1))
                                if hv < 2:
                                    w = 16 * S2 + 7 + hv
                                else:
                                    w = 16 * S2 + 7 + hv
                                te.wait_ge(sDVkv, max(0, w))
                                for c in range(NDC):
                                    ins = te.matmul(pC[2 + hv % 2][:, :],
                                                    xnTb[sp][:, c * STR + tb * 128: c * STR + (tb + 1) * 128],
                                                    wv_sb[:, c * 1024 + half * 512: c * 1024 + (half + 1) * 512],
                                                    start=(c == 0), stop=(c == NDC - 1))
                                if hv < HP - 1:
                                    ins.then_inc(sPEkvp, 1)
                                else:
                                    ins.then_inc(sPEx, 1)
                    if pa == 0:
                        te.wait_ge(sTOwn, 16 * (NDC + 1))
                    te.wait_ge(sDVkv, 16 * (NSTR * (pa + 1) - 1) + 6)
                    for qb in range(16):
                        gq = 16 * pa + qb
                        hrel = qb // 2
                        ist = qb % 2
                        if qb == 0:
                            te.wait_ge(sWq, 16 * (pa + 1))
                        if qb == 8:
                            te.wait_ge(sWqB, 16 * (pa + 1))
                        if qb >= 2:
                            te.wait_ge(sDVq, gq - 1)
                        for c in range(NDC):
                            ins = te.matmul(pC[qb % 2][:, :],
                                            wk_sb[:, c * 1024 + hrel * 128: c * 1024 + (hrel + 1) * 128],
                                            xnT_own[:, c * OWN + ist * STR: c * OWN + (ist + 1) * STR],
                                            start=(c == 0), stop=(c == NDC - 1))
                        ins.then_inc(sPEqs, 1)

        with ExitStack() as sb1:
            kT_sb = [sb1.enter_context(nc.sbuf_tensor(f"kT{i}", [128, NT], BF16)) for i in range(2)]
            vb = [sb1.enter_context(nc.sbuf_tensor(f"vb{i}", [128, 64 * VSLOT], BF16)) for i in range(2)]
            PT = [sb1.enter_context(nc.sbuf_tensor(f"PT{i}", [128, 512], BF16)) for i in range(2)]
            ctx_sb = [sb1.enter_context(nc.sbuf_tensor(f"ctx{i}", [128, 1024], BF16)) for i in range(2)]
            craw = [sb1.enter_context(nc.sbuf_tensor(f"craw{i}", [128, 1024], F32)) for i in range(2)]
            rec_sb = sb1.enter_context(nc.sbuf_tensor("rec_sb", [128, 4], F32))
            wg_sb = [sb1.enter_context(nc.sbuf_tensor(f"wg{i}", [128, 2048], BF16)) for i in range(2)]
            wu_sb = [sb1.enter_context(nc.sbuf_tensor(f"wu{i}", [128, 2048], BF16)) for i in range(2)]
            esb = [sb1.enter_context(nc.sbuf_tensor(f"esb{i}", [128, 512], F32)) for i in range(2)]
            rsb = [sb1.enter_context(nc.sbuf_tensor(f"rsb{i}", [128, 512], F32)) for i in range(2)]
            gated_sb = [sb1.enter_context(nc.sbuf_tensor(f"gat{i}", [128, 1024], BF16)) for i in range(2)]

            Sbuf = [pS[0][:, 0:512], pS[0][:, 512:1024]]
            ctxbank = [(pC[0], 0), (pC[0], 256), (pC[1], 0), (pC[1], 256)]
            Gb = [pS[1][:, 0:512], pS[1][:, 512:1024]]
            Ub = [pC[2][:, :], pC[3][:, :]]

            blkB1 = sb1.enter_context(nc.Block())

            A_PE = None

            @blkB1.gpsimd
            def _(g):
                g.wait_ge(sDVq, 16 * NP)
                g.wait_ge(sKW[0], 16 * (8 * NSTR * NP))
                g.wait_ge(sKW[1], 16 * (8 * NSTR * NP))
                for sl2 in range(3):
                    g.wait_ge(sXW[sl2], 16 * ((NTILE - sl2 + 2) // 3))
                g.wait_ge(sTOwn, 16 * (NDC + 1))
                for h in range(NH):
                    hp = h % 2
                    if h >= 2:
                        g.wait_ge(sPEpv, 2 * (h - 1))
                    g.dma_start(kT_sb[hp][:, :], kT_scr[h, :, :]).then_inc(sKL[hp], 16)
                    vsrc = v_scr[:, h, :].rearrange("(jc p) c -> p jc c", p=128)
                    vdst = vb[hp][:, :].rearrange("p (jc w) -> p jc w", w=VSLOT)[:, :, 0:128]
                    for part in range(8):
                        g.dma_start(vdst[:, part * 8:(part + 1) * 8, :],
                                    vsrc[:, part * 8:(part + 1) * 8, :]).then_inc(sVL[hp], 16)
                    for b in range(4):
                        hb = 4 * h + b
                        bp = hb % 2
                        if hb >= 2:
                            g.wait_ge(sPEg, 2 * hb - 2)
                        g.dma_start(wg_sb[bp][:, :], wgb[hb, :, :]).then_inc(sWgu[bp], 16)
                        g.dma_start(wu_sb[bp][:, :], wub[hb, :, :]).then_inc(sWgu[bp], 16)

            @blkB1.vector
            def _(v):
                v.wait_ge(sPEqs, 16 * NP)
                for i in range(2):
                    ones_view = vb[i][:, :].rearrange("p (s w) -> p s w", w=VSLOT)[:, :, 128:129]
                    ins = v.memset(ones_view, 1.0)
                ins.then_inc(sINIT, 1)
                for h in range(NH):
                    for ist in range(2):
                        u = h * 2 + ist
                        cp2 = u % 2

                        def gate_mult(seg):
                            gsb = u * 4 + seg
                            sp2 = gsb % 2
                            hb = gsb // 2
                            half = gsb % 2
                            v.wait_ge(sSil, gsb + 1)
                            if half == 0 and hb >= 2:
                                v.wait_ge(sGW[hb % 2], 16 * (hb // 2))
                            v.tensor_scalar_add(rsb[sp2][:, :], esb[sp2][:, :], cst[:, 5:6])
                            v.drain()
                            v.reciprocal(esb[sp2][:, :], rsb[sp2][:, :])
                            v.drain()
                            v.tensor_tensor(rsb[sp2][:, :], Gb[sp2], esb[sp2][:, :], ALU.mult)
                            v.drain()
                            v.tensor_tensor(gated_sb[hb % 2][:, half * 512:(half + 1) * 512],
                                            rsb[sp2][:, :], Ub[sp2],
                                            ALU.mult).then_inc(sDVgu, 1)
                            v.drain()

                        for seg in range(3):
                            gate_mult(seg)
                        v.wait_ge(sPEpv, u + 1)
                        v.drain()
                        v.tensor_copy(craw[cp2][:, 0:512], pC[0][:, :]).then_inc(sDVctx, 1)
                        v.tensor_copy(craw[cp2][:, 512:1024], pC[1][:, :]).then_inc(sDVctx, 1)
                        v.drain()
                        gate_mult(3)
                        if h >= 2 and ist == 0:
                            v.wait_ge(sCTXW[h % 2], 16 * (h // 2))
                        for ib in range(4):
                            base = (ib // 2) * 512 + (ib % 2) * 256
                            v.reciprocal(rec_sb[:, 0:1], craw[cp2][:, base + 128:base + 129])
                            v.drain()
                            col = ist * 512 + ib * 128
                            v.tensor_scalar_mul(ctx_sb[h % 2][:, col:col + 128],
                                                craw[cp2][:, base:base + 128],
                                                rec_sb[:, 0:1]).then_inc(sDVN, 1)
                            v.drain()

            @blkB1.scalar
            def _(sc):
                sc.wait_ge(sINIT, 2)
                for h in range(NH):
                    for ist in range(2):
                        u = h * 2 + ist
                        for jc in range(64):
                            gc = u * 64 + jc
                            pp = jc % 2
                            sc.wait_ge(sS, gc + 1)
                            if gc >= 2:
                                if jc == 0:
                                    sc.wait_ge(sPVd, u * 63)
                                elif jc == 1:
                                    sc.wait_ge(sPEpv, u)
                                else:
                                    sc.wait_ge(sPVd, u * 63 + jc - 1)
                            sc.activation(PT[pp][:, :], Sbuf[pp], AF.Exp,
                                          scale=cst[:, 1:2], bias=cst[:, 2:3]).then_inc(sEXP[pp], 1)
                            if jc % 16 == 15:
                                gsb = u * 4 + jc // 16
                                sp2 = gsb % 2
                                sc.wait_ge(sPEg, gsb + 1)
                                if gsb >= 2:
                                    sc.wait_ge(sDVgu, gsb - 1)
                                sc.activation(esb[sp2][:, :], Gb[sp2], AF.Exp,
                                              scale=cst[:, 4:5]).then_inc(sSil, 1)

            @blkB1.sync
            def _(sy):
                for h in range(NH):
                    for b in range(4):
                        hb = 4 * h + b
                        sy.wait_ge(sDVgu, 2 * hb + 2)
                        if hb >= 2:
                            sy.wait_ge(sGW[hb % 2], 16 * (hb // 2))
                        sy.dma_start(gated_scr[hb, :, :],
                                     gated_sb[hb % 2][:, :]).then_inc(sGW[hb % 2], 16)
                    sy.wait_ge(sDVN, 4 * (2 * h + 2))
                    if h >= 2:
                        sy.wait_ge(sCTXW[h % 2], 16 * (h // 2))
                    src = ctx_sb[h % 2][:, :].rearrange("p (b c) -> p b c", c=128)
                    dst = ctx_nat[:, :].rearrange("(b p) (hh c) -> hh p b c", p=128, c=128)[h]
                    sy.dma_start(dst, src).then_inc(sCTXW[h % 2], 16)

            @blkB1.tensor
            def _(te):
                te.wait_ge(sINIT, 2)
                for h in range(NH):
                    hp = h % 2
                    te.wait_ge(sKL[hp], 16 * (h // 2 + 1))
                    te.wait_ge(sVL[hp], 128 * (h // 2 + 1))
                    for ist in range(2):
                        u = h * 2 + ist
                        qslice = qT[:, h * OWN + ist * STR: h * OWN + (ist + 1) * STR]

                        def s_mm(jc):
                            gc = u * 64 + jc
                            pp = jc % 2
                            if gc >= 2:
                                te.wait_ge(sEXP[pp], gc // 2)
                            te.matmul(Sbuf[pp],
                                      kT_sb[hp][:, jc * 128:(jc + 1) * 128],
                                      qslice, start=True, stop=True).then_inc(sS, 1)

                        s_mm(0)
                        for jc in range(64):
                            gc = u * 64 + jc
                            pp = jc % 2
                            seg = jc // 16
                            gsb = u * 4 + seg
                            sp2 = gsb % 2
                            hb = gsb // 2
                            half = gsb % 2
                            bp = hb % 2
                            if jc < 63:
                                s_mm(jc + 1)
                            if jc % 16 == 0:
                                te.wait_ge(sWgu[bp], 32 * (hb // 2 + 1))
                                if gsb >= 2:
                                    te.wait_ge(sSil, gsb - 1)
                                    te.wait_ge(sDVgu, gsb - 1)
                            for m in (2 * (jc % 16), 2 * (jc % 16) + 1):
                                if m < 16:
                                    te.matmul(Gb[sp2],
                                              wg_sb[bp][:, m * 128:(m + 1) * 128],
                                              xnT_own[:, m * OWN + half * 512: m * OWN + (half + 1) * 512],
                                              start=(m == 0), stop=(m == 15))
                                else:
                                    c = m - 16
                                    ins_g = te.matmul(Ub[sp2],
                                                      wu_sb[bp][:, c * 128:(c + 1) * 128],
                                                      xnT_own[:, c * OWN + half * 512: c * OWN + (half + 1) * 512],
                                                      start=(c == 0), stop=(c == 15))
                                    if c == 15:
                                        ins_g.then_inc(sPEg, 1)
                            if jc == 0:
                                te.wait_ge(sDVctx, 2 * u)
                            te.wait_ge(sEXP[pp], gc // 2 + 1)
                            for ib in range(4):
                                bank, boff = ctxbank[ib]
                                ins = te.matmul(bank[:, boff:boff + 129],
                                                PT[pp][:, ib * 128:(ib + 1) * 128],
                                                vb[hp][:, jc * VSLOT: jc * VSLOT + 129],
                                                start=(jc == 0 and boff == 0),
                                                stop=(jc == 63 and boff == 256))
                            if jc < 63:
                                ins.then_inc(sPVd, 1)
                            else:
                                ins.then_inc(sPEpv, 1)

        with ExitStack() as sb2:
            gatedin = [sb2.enter_context(nc.sbuf_tensor(f"gin{i}", [128, 4096], BF16)) for i in range(3)]
            wd_sb = [sb2.enter_context(nc.sbuf_tensor(f"wd{i}", [128, 2048], BF16)) for i in range(3)]
            wpq_sb = [sb2.enter_context(nc.sbuf_tensor(f"wpq{i}", [128, 8192], BF16)) for i in range(2)]
            ctxT_sb = sb2.enter_context(nc.sbuf_tensor("ctxT_sb", [128, 16 * OWN], BF16,
                                                       is_transpose_dest=True))
            xo_sb = [sb2.enter_context(nc.sbuf_tensor(f"xo{i}", [128, 512], F32)) for i in range(4)]
            out_sb = [sb2.enter_context(nc.sbuf_tensor(f"os{i}", [128, 512], F32)) for i in range(2)]
            bias_sb = sb2.enter_context(nc.sbuf_tensor("bias_sb", [128, D], F32))

            banks = [pS[0][:, 0:512], pS[0][:, 512:1024],
                     pS[1][:, 0:512], pS[1][:, 512:1024],
                     pC[0][:, :], pC[1][:, :], pC[2][:, :], pC[3][:, :]]

            blkB2 = sb2.enter_context(nc.Block())

            def b2_barrier(eng):
                eng.wait_ge(sPEg, 128)
                eng.wait_ge(sSil, 128)
                eng.wait_ge(sDVgu, 128)
                eng.wait_ge(sDVctx, 64)
                eng.wait_ge(sDVN, 128)
                eng.wait_ge(sEXP[0], 1024)
                eng.wait_ge(sEXP[1], 1024)
                eng.wait_ge(sCTXW[0], 16 * 8)
                eng.wait_ge(sCTXW[1], 16 * 8)
                eng.wait_ge(sGW[0], 16 * 32)
                eng.wait_ge(sGW[1], 16 * 32)

            @blkB2.gpsimd
            def _(g):
                b2_barrier(g)
                g.wait_ge(sWC, 16 * (3 * NP - 2) + 128)
                g.dma_start(bias_sb[:, :], biasb[:, :]).then_inc(sBB, 16)
                for q in range(4):
                    if q >= 2:
                        g.wait_ge(sPEob, 8 * (q - 1))
                    g.dma_start(wpq_sb[q % 2][:, :], wpb[q, :, :]).then_inc(sWp[q % 2], 16)
                    for grp in range(16):
                        n = 16 * q + grp
                        cp = n % 3
                        if n >= 3:
                            g.wait_ge(sPEhc, n - 2)
                        g.dma_start(wd_sb[cp][:, :].rearrange("p (s f) -> p s f", s=4),
                                    wdb[q, 4 * grp:4 * grp + 4, :, :].rearrange(
                                        "s p f -> p s f")).then_inc(sGin[cp], 16)

            @blkB2.sync
            def _(sy):
                b2_barrier(sy)
                for hc in range(NDC):
                    sy.dma_start(ctxT_sb[:, hc * OWN:(hc + 1) * OWN],
                                 ctx_nat[:, hc * 128:(hc + 1) * 128],
                                 transpose=True).then_inc(sTctx, 16)
                for q in range(4):
                    for grp in range(16):
                        n = 16 * q + grp
                        cp = n % 3
                        if n >= 3:
                            sy.wait_ge(sPEhc, n - 2)
                        sy.dma_start(gatedin[cp][:, :].rearrange("p (s f) -> p s f", s=4),
                                     gated_scr[4 * grp:4 * grp + 4, :, :].rearrange(
                                         "s p f -> p s f")).then_inc(sGR[cp], 16)
            @blkB2.scalar
            def _(sc):
                b2_barrier(sc)

                def xo_load(n):
                    q, tb = n // 8, n % 8
                    sc.dma_start(xo_sb[n % 4][:, :],
                                 x_own[tb * 128:(tb + 1) * 128, q * 512:(q + 1) * 512]
                                 ).then_inc(sXO[n % 4], 16)

                for n0 in range(4):
                    xo_load(n0)
                for n in range(32):
                    q, tb = n // 8, n % 8
                    sc.wait_ge(sDVo, n + 1)
                    if n >= 2:
                        sc.wait_ge(sOUT[tb % 2], 16 * (n // 2))
                    sc.dma_start(out[tb * 128:(tb + 1) * 128, q * 512:(q + 1) * 512],
                                 out_sb[tb % 2][:, :]).then_inc(sOUT[tb % 2], 16)
                    if n + 4 < 32:
                        xo_load(n + 4)
                sc.wait_ge(sOUT[0], 16 * 16)
                sc.wait_ge(sOUT[1], 16 * 16)

            @blkB2.tensor
            def _(te):
                for q in range(4):
                    for grp in range(16):
                        n = 16 * q + grp
                        cp = n % 3
                        te.wait_ge(sGin[cp], 16 * (n // 3 + 1))
                        te.wait_ge(sGR[cp], 16 * (n // 3 + 1))
                        for sub in range(4):
                            for tb in range(8):
                                if grp == 0 and sub == 0 and q >= 1:
                                    te.wait_ge(sDVo, 8 * (q - 1) + tb + 1)
                                ins = te.matmul(banks[tb],
                                                gatedin[cp][:, sub * 1024 + tb * 128: sub * 1024 + (tb + 1) * 128],
                                                wd_sb[cp][:, sub * 512:(sub + 1) * 512],
                                                start=(grp == 0 and sub == 0), stop=False)
                        ins.then_inc(sPEhc, 1)
                    te.wait_ge(sWp[q % 2], 16 * (q // 2 + 1))
                    if q == 0:
                        te.wait_ge(sTctx, 16 * NDC)
                    for hc2 in range(NDC):
                        for tb in range(8):
                            ins = te.matmul(banks[tb],
                                            ctxT_sb[:, hc2 * OWN + tb * 128: hc2 * OWN + (tb + 1) * 128],
                                            wpq_sb[q % 2][:, hc2 * 512:(hc2 + 1) * 512],
                                            start=False, stop=(hc2 == NDC - 1))
                            if hc2 == NDC - 1:
                                ins.then_inc(sPEob, 1)

            @blkB2.vector
            def _(v):
                v.wait_ge(sBB, 16)

                def tt1(n):
                    q, tb = n // 8, n % 8
                    tp = tb % 2
                    v.wait_ge(sXO[n % 4], 16 * (n // 4 + 1))
                    if n >= 2:
                        v.wait_ge(sOUT[tp], 16 * (n // 2))
                    v.tensor_tensor(out_sb[tp][:, :], xo_sb[n % 4][:, :],
                                    bias_sb[:, q * 512:(q + 1) * 512], ALU.add)
                    v.drain()

                tt1(0)
                tt1(1)
                for n in range(32):
                    q, tb = n // 8, n % 8
                    tp = tb % 2
                    v.wait_ge(sPEob, n + 1)
                    v.tensor_tensor(out_sb[tp][:, :], out_sb[tp][:, :],
                                    banks[tb], ALU.add).then_inc(sDVo, 1)
                    v.drain()
                    if n + 2 < 32:
                        tt1(n + 2)

    return nc


def kernel(L0_attn_query, L0_attn_key, L0_attn_value, L0_attn_Wqkv, L0_attn_Wproj,
           L0_attn_bproj, L0_ffn_w_gate, L0_ffn_w_up, L0_ffn_w_down):
    x = np.ascontiguousarray(np.asarray(L0_attn_query, dtype=np.float32).reshape(NT, D))
    Wqkv = np.asarray(L0_attn_Wqkv, dtype=np.float32)
    Wproj = np.asarray(L0_attn_Wproj, dtype=np.float32)
    bproj = np.asarray(L0_attn_bproj, dtype=np.float32)
    w_gate = np.asarray(L0_ffn_w_gate, dtype=np.float32)
    w_up = np.asarray(L0_ffn_w_up, dtype=np.float32)
    w_down = np.asarray(L0_ffn_w_down, dtype=np.float32)

    Wr = Wqkv.reshape(NH, 3, 128, D)
    wq = Wr[:, 0].reshape(NH * 128, D)
    wk = Wr[:, 1].reshape(NH * 128, D)
    wv = Wr[:, 2].reshape(NH * 128, D)

    def per_pass(w):
        return np.ascontiguousarray(
            w.T.reshape(16, 128, NP, HP * 128).transpose(2, 1, 0, 3)).reshape(NP, 128, HP * 2048)

    ins = dict(
        x=x,
        wqp=per_pass(wq), wkp=per_pass(wk), wvp=per_pass(wv),
        wgb=np.ascontiguousarray(
            w_gate.T.reshape(16, 128, 64, 128).transpose(2, 1, 0, 3)).reshape(64, 128, 2048),
        wub=np.ascontiguousarray(
            w_up.T.reshape(16, 128, 64, 128).transpose(2, 1, 0, 3)).reshape(64, 128, 2048),
        wdq=np.ascontiguousarray(w_down.T.reshape(64, 128, 4, 512).transpose(2, 0, 1, 3)),
        wpq=np.ascontiguousarray(
            Wproj.T.reshape(16, 128, 4, 512).transpose(2, 1, 0, 3)).reshape(4, 128, 8192),
        biasb=np.ascontiguousarray(np.broadcast_to(bproj[None, :], (128, D))),
    )

    nc = build_nc()
    in_maps = []
    for c in range(8):
        m = dict(ins)
        m["x_own"] = np.ascontiguousarray(x[c * OWN:(c + 1) * OWN])
        in_maps.append(m)
    res = run_bass_kernel_spmd(nc, in_maps, core_ids=list(range(8)))
    full = np.concatenate([res.results[c]["out"] for c in range(8)], axis=0)
    return full.reshape(1, NT, D)


# version 17
# speedup vs baseline: 1.1043x; 1.0022x over previous
import numpy as np
from contextlib import ExitStack

import concourse.bass as bass
import concourse.mybir as mybir
from concourse.bass_utils import run_bass_kernel_spmd

F32 = mybir.dt.float32
BF16 = mybir.dt.bfloat16
AF = mybir.ActivationFunctionType
AX = mybir.AxisListType
ALU = mybir.AluOpType

D = 2048
NT = 8192
OWN = 1024
NH = 16
NP = 2
HP = 8
NDC = 16
NSTR = 16
STR = 512
NTILE = 64
NHB = 64
VSLOT = 132

INVD = 1.0 / 2048.0
EPS = 1e-5
QKSCALE = 0.08838834764831845
CSHIFT = 4.0


def build_nc():
    nc = bass.Bass("TRN2", target_bir_lowering=False)

    x = nc.dram_tensor("x", [NT, D], F32, kind="ExternalInput")
    x_own = nc.dram_tensor("x_own", [OWN, D], F32, kind="ExternalInput")
    wqp = nc.dram_tensor("wqp", [NP, 128, HP * 2048], F32, kind="ExternalInput")
    wkp = nc.dram_tensor("wkp", [NP, 128, HP * 2048], F32, kind="ExternalInput")
    wvp = nc.dram_tensor("wvp", [NP, 128, HP * 2048], F32, kind="ExternalInput")
    wgb = nc.dram_tensor("wgb", [NHB, 128, 2048], F32, kind="ExternalInput")
    wub = nc.dram_tensor("wub", [NHB, 128, 2048], F32, kind="ExternalInput")
    wdq = nc.dram_tensor("wdq", [4, 64, 128, 512], F32, kind="ExternalInput")
    wpq = nc.dram_tensor("wpq", [4, 128, 8192], F32, kind="ExternalInput")
    biasb = nc.dram_tensor("biasb", [128, D], F32, kind="ExternalInput")
    out = nc.dram_tensor("out", [OWN, D], F32, kind="ExternalOutput")

    xn_nat = nc.dram_tensor("xn_nat", [NT, D], BF16, kind="Internal")
    wkb = nc.dram_tensor("wkb", [NP, 128, HP * 2048], BF16, kind="Internal")
    wvb = nc.dram_tensor("wvb", [NP, 128, HP * 2048], BF16, kind="Internal")
    wqb = nc.dram_tensor("wqb", [NP, 128, HP * 2048], BF16, kind="Internal")
    wdb = nc.dram_tensor("wdb", [4, 64, 128, 512], BF16, kind="Internal")
    wpb = nc.dram_tensor("wpb", [4, 128, 8192], BF16, kind="Internal")
    xn_own_nat = nc.dram_tensor("xn_own_nat", [OWN, D], BF16, kind="Internal")
    kT_scr = nc.dram_tensor("kT_scr", [NH, 128, NT], BF16, kind="Internal")
    v_scr = nc.dram_tensor("v_scr", [NT, NH, 128], BF16, kind="Internal")
    ctx_nat = nc.dram_tensor("ctx_nat", [OWN, D], BF16, kind="Internal")
    gated_scr = nc.dram_tensor("gated_scr", [NHB, 128, OWN], BF16, kind="Internal")

    st = ExitStack()
    with st:
        xnT_own = st.enter_context(nc.sbuf_tensor("xnT_own", [128, NDC * OWN], BF16))
        qT = st.enter_context(nc.sbuf_tensor("qT", [128, NH * OWN], BF16))
        cst = st.enter_context(nc.sbuf_tensor("cst", [128, 8], F32))
        pS = [st.enter_context(nc.psum_tensor(f"pS{i}", [128, 1024], F32)) for i in range(2)]
        pC = [st.enter_context(nc.psum_tensor(f"pC{i}", [128, 512], F32)) for i in range(4)]

        def sem(name):
            return st.enter_context(nc.semaphore(name))

        sXin = [sem("sXin0"), sem("sXin1")]
        sSq = sem("sSq")
        sVar = sem("sVar")
        sStd = sem("sStd")
        sNR = sem("sNR")
        sLN = sem("sLN")
        sXW = [sem(f"sXW{i}") for i in range(3)]
        sTOwn = sem("sTOwn")
        sTR = [sem("sTR0"), sem("sTR1")]
        sPEx = sem("sPEx")
        sWk = sem("sWk")
        sWv = sem("sWv")
        sWq = sem("sWq")
        sWkB = sem("sWkB")
        sWvB = sem("sWvB")
        sWqB = sem("sWqB")
        sPEkvp = sem("sPEkvp")
        sDVkv = sem("sDVkv")
        sKW = [sem("sKW0"), sem("sKW1")]
        sPEqs = sem("sPEqs")
        sDVq = sem("sDVq")
        sKL = [sem("sKL0"), sem("sKL1")]
        sVL = [sem("sVL0"), sem("sVL1")]
        sS = sem("sS")
        sEXP = [sem("sEXP0"), sem("sEXP1")]
        sPVd = sem("sPVd")
        sPEpv = sem("sPEpv")
        sDVctx = sem("sDVctx")
        sDVN = sem("sDVN")
        sCTXW = [sem("sCTXW0"), sem("sCTXW1")]
        sWgu = [sem("sWgu0"), sem("sWgu1")]
        sPEg = sem("sPEg")
        sSil = sem("sSil")
        sDVgu = sem("sDVgu")
        sGW = [sem("sGW0"), sem("sGW1")]
        sTctx = sem("sTctx")
        sTctx2 = sem("sTctx2")
        sWp = [sem("sWp0"), sem("sWp1")]
        sGin = [sem(f"sGin{i}") for i in range(3)]
        sGR = [sem(f"sGR{i}") for i in range(3)]
        sPEhc = sem("sPEhc")
        sPEob = sem("sPEob")
        sXO = [sem(f"sXO{i}") for i in range(4)]
        sDVo = sem("sDVo")
        sOUT = [sem("sOUT0"), sem("sOUT1")]
        sBB = sem("sBB")
        sINIT = sem("sINIT")
        sWC = sem("sWC")

        with ExitStack() as sa:
            xinp = [sa.enter_context(nc.sbuf_tensor(f"xinp{i}", [128, 2 * D], BF16)) for i in range(2)]
            stats = [sa.enter_context(nc.sbuf_tensor(f"stats{i}", [128, 16], F32)) for i in range(4)]
            xn_bf = [sa.enter_context(nc.sbuf_tensor(f"xnbf{i}", [128, D], BF16)) for i in range(3)]
            xnTb = [sa.enter_context(nc.sbuf_tensor(f"xnTb{i}", [128, NDC * STR], BF16,
                                                    is_transpose_dest=True)) for i in range(2)]
            wk_sb = sa.enter_context(nc.sbuf_tensor("wk_sb", [128, HP * 2048], BF16))
            wv_sb = sa.enter_context(nc.sbuf_tensor("wv_sb", [128, HP * 2048], BF16))
            kst = [sa.enter_context(nc.sbuf_tensor(f"kst{i}", [128, 512], BF16)) for i in range(2)]
            vst = kst

            blkA = sa.enter_context(nc.Block())

            def whalf(ap, hf):
                return ap.rearrange("p (c w) -> p c w", w=1024)[:, :, hf * 512:(hf + 1) * 512]

            @blkA.gpsimd
            def _(g):
                def x_pair(j):
                    jp = j % 2
                    if j >= 2:
                        g.wait_ge(sLN, 2 * j - 2)
                    g.dma_start(xinp[jp][:, :].rearrange("p (s f) -> p s f", s=2),
                                x[j * 256:(j + 1) * 256, :].rearrange("(s p) f -> p s f", p=128)
                                ).then_inc(sXin[jp], 16)

                x_pair(0)
                x_pair(1)
                g.dma_start(whalf(wk_sb[:, :], 0), whalf(wkp[0], 0)).then_inc(sWk, 16)
                x_pair(2)
                g.dma_start(whalf(wk_sb[:, :], 1), whalf(wkp[0], 1)).then_inc(sWkB, 16)
                x_pair(3)
                g.dma_start(whalf(wv_sb[:, :], 0), whalf(wvp[0], 0)).then_inc(sWv, 16)
                x_pair(4)
                g.dma_start(whalf(wv_sb[:, :], 1), whalf(wvp[0], 1)).then_inc(sWvB, 16)
                for j in range(5, NTILE // 2):
                    x_pair(j)
                g.dma_start(wqb[0, :, :], wqp[0, :, :]).then_inc(sWC, 16)
                for pa in range(1, NP):
                    g.dma_start(wkb[pa, :, :], wkp[pa, :, :]).then_inc(sWC, 16)
                    g.dma_start(wvb[pa, :, :], wvp[pa, :, :]).then_inc(sWC, 16)
                    g.dma_start(wqb[pa, :, :], wqp[pa, :, :]).then_inc(sWC, 16)
                g.wait_ge(sWC, 16 * (3 * NP - 2))
                for pa in range(NP):
                    g.wait_ge(sPEkvp, 15 * (NSTR * (pa + 1) - 1) + 8)
                    g.dma_start(whalf(wk_sb[:, :], 0), whalf(wqb[pa], 0)).then_inc(sWq, 16)
                    g.dma_start(whalf(wk_sb[:, :], 1), whalf(wqb[pa], 1)).then_inc(sWqB, 16)
                    if pa + 1 < NP:
                        g.wait_ge(sPEx, NSTR * (pa + 1))
                        g.dma_start(whalf(wv_sb[:, :], 0), whalf(wvb[pa + 1], 0)).then_inc(sWv, 16)
                        g.wait_ge(sPEqs, 16 * pa + 8)
                        g.dma_start(whalf(wk_sb[:, :], 0), whalf(wkb[pa + 1], 0)).then_inc(sWk, 16)
                        g.dma_start(whalf(wv_sb[:, :], 1), whalf(wvb[pa + 1], 1)).then_inc(sWvB, 16)
                        g.wait_ge(sPEqs, 16 * (pa + 1))
                        g.dma_start(whalf(wk_sb[:, :], 1), whalf(wkb[pa + 1], 1)).then_inc(sWkB, 16)

            @blkA.vector
            def _(v):
                v.memset(cst[:, 0:1], EPS)
                v.memset(cst[:, 1:2], QKSCALE)
                v.memset(cst[:, 2:3], -CSHIFT)
                v.memset(cst[:, 3:4], INVD)
                v.memset(cst[:, 4:5], -1.0)
                v.memset(cst[:, 5:6], 1.0).then_inc(sINIT, 1)

                def ln_tile(t):
                    sl = t % 4
                    S = stats[sl]
                    xv = xinp[(t // 2) % 2][:, (t % 2) * D:(t % 2 + 1) * D]
                    v.wait_ge(sXin[(t // 2) % 2], 16 * (t // 4 + 1))
                    v.reduce_sum(S[:, 0:1], xv, axis=AX.X)
                    v.drain()
                    v.wait_ge(sSq, t + 1)
                    v.tensor_scalar_mul(S[:, 2:3], S[:, 0:1], cst[:, 3:4])
                    v.drain()
                    v.tensor_mul(S[:, 3:4], S[:, 2:3], S[:, 2:3])
                    v.drain()
                    v.scalar_tensor_tensor(S[:, 4:5], S[:, 1:2], cst[:, 3:4], S[:, 3:4],
                                           ALU.mult, ALU.subtract).then_inc(sVar, 1)
                    v.wait_ge(sStd, t + 1)
                    v.reciprocal(S[:, 6:7], S[:, 5:6])
                    v.drain()
                    v.scalar_tensor_tensor(S[:, 7:8], S[:, 2:3], cst[:, 4:5], S[:, 6:7],
                                           ALU.mult, ALU.mult).then_inc(sNR, 1)
                    v.drain()
                    v.tensor_scalar(xn_bf[t % 3][:, :], xv, S[:, 6:7], S[:, 7:8],
                                    ALU.mult, ALU.add).then_inc(sLN, 1)

                LN_SLOTS = {1: 0, 2: 1, 5: 2, 8: 3}

                def kv_evac_stripe(pa, s, lns=()):
                    S2 = pa * NSTR + s
                    for i in range(2 * HP):
                        if i in LN_SLOTS and LN_SLOTS[i] < len(lns):
                            ln_tile(lns[LN_SLOTS[i]])
                        n = 16 * S2 + i
                        if i < HP:
                            kb = i
                            v.wait_ge(sPEkvp, 15 * S2 + kb + 1)
                            if n >= 2:
                                v.wait_ge(sKW[n % 2], 16 * (n // 2))
                            v.tensor_copy(kst[n % 2][:, :], pC[kb % 4][:, :]).then_inc(sDVkv, 1)
                        else:
                            hv = i - HP
                            if hv < HP - 1:
                                v.wait_ge(sPEkvp, 15 * S2 + HP + hv + 1)
                            else:
                                v.wait_ge(sPEx, S2 + 1)
                            v.wait_ge(sKW[n % 2], 16 * (n // 2))
                            v.tensor_copy(vst[n % 2][:, :], pC[2 + hv % 2][:, :]).then_inc(sDVkv, 1)

                def q_evac(pa):
                    for qb in range(16):
                        gq = 16 * pa + qb
                        v.wait_ge(sPEqs, gq + 1)
                        h = pa * HP + qb // 2
                        ist = qb % 2
                        v.tensor_copy(qT[:, h * OWN + ist * STR: h * OWN + (ist + 1) * STR],
                                      pC[qb % 2][:, :]).then_inc(sDVq, 1)

                LOOK = 12
                for tt in range(LOOK):
                    ln_tile(tt)
                for s in range(NSTR):
                    kv_evac_stripe(0, s,
                                   list(range(4 * s + LOOK, min(4 * s + LOOK + 4, NTILE))))
                q_evac(0)
                for pa in range(1, NP):
                    for s in range(NSTR):
                        kv_evac_stripe(pa, s)
                    q_evac(pa)

            @blkA.scalar
            def _(sc):
                sc.wait_ge(sINIT, 1)

                def ln_acts(t):
                    sl = t % 4
                    S = stats[sl]
                    xv = xinp[(t // 2) % 2][:, (t % 2) * D:(t % 2 + 1) * D]
                    sc.wait_ge(sXin[(t // 2) % 2], 16 * (t // 4 + 1))
                    if t >= 4:
                        sc.wait_ge(sLN, t - 3)
                    if t >= 3:
                        sc.wait_ge(sLN, t - 2)
                        sc.wait_ge(sXW[t % 3], 16 * (t // 3))
                    sc.drain()
                    sc.activation(xn_bf[t % 3][:, :], xv, AF.Square,
                                  accum_out=S[:, 1:2]).then_inc(sSq, 1)
                    sc.wait_ge(sVar, t + 1)
                    sc.activation(S[:, 5:6], S[:, 4:5], AF.Sqrt,
                                  bias=cst[:, 0:1]).then_inc(sStd, 1)

                LN_SLOTS = {1: 0, 2: 1, 5: 2, 8: 3}

                def kv_write_stripe(pa, s, lns=()):
                    S2 = pa * NSTR + s
                    for i in range(2 * HP):
                        if i in LN_SLOTS and LN_SLOTS[i] < len(lns):
                            ln_acts(lns[LN_SLOTS[i]])
                        n = 16 * S2 + i
                        sc.wait_ge(sDVkv, n + 1)
                        if n >= 2:
                            sc.wait_ge(sKW[n % 2], 16 * (n // 2))
                        if i < HP:
                            h = pa * HP + i
                            sc.dma_start(kT_scr[h, :, s * STR:(s + 1) * STR],
                                         kst[n % 2][:, :]).then_inc(sKW[n % 2], 16)
                        else:
                            hv = i - HP
                            tb, half = hv // 2, hv % 2
                            dst = v_scr[s * STR + tb * 128: s * STR + (tb + 1) * 128,
                                        pa * HP + half * 4: pa * HP + (half + 1) * 4, :]
                            src = vst[n % 2][:, :].rearrange("p (h c) -> p h c", c=128)
                            sc.dma_start(dst, src).then_inc(sKW[n % 2], 16)

                LOOK = 12
                for tt in range(LOOK):
                    ln_acts(tt)
                for s in range(NSTR):
                    kv_write_stripe(0, s,
                                    list(range(4 * s + LOOK, min(4 * s + LOOK + 4, NTILE))))
                for pa in range(1, NP):
                    for s in range(NSTR):
                        kv_write_stripe(pa, s)

            @blkA.sync
            def _(sy):
                for t in range(NTILE):
                    sl = t % 3
                    sy.wait_ge(sLN, t + 1)
                    if t >= 3:
                        sy.wait_ge(sXW[sl], 16 * (t // 3))
                    sy.dma_start(xn_nat[t * 128:(t + 1) * 128, :], xn_bf[sl][:, :]).then_inc(sXW[sl], 16)
                    if t % 4 == 3 and t >= 4:
                        ss = t // 4 - 1
                        sp = ss % 2
                        for sl2 in range(3):
                            sy.wait_ge(sXW[sl2], 16 * ((4 * ss + 6 - sl2) // 3))
                        if ss >= 2:
                            sy.wait_ge(sPEx, ss - 1)
                        for c in range(NDC):
                            sy.dma_start(xnTb[sp][:, c * STR:(c + 1) * STR],
                                         xn_nat[ss * STR:(ss + 1) * STR, c * 128:(c + 1) * 128],
                                         transpose=True).then_inc(sTR[sp], 16)
                if True:
                    ss = NSTR - 1
                    sp = ss % 2
                    for sl2 in range(3):
                        sy.wait_ge(sXW[sl2], 16 * ((4 * ss + 6 - sl2) // 3))
                    sy.wait_ge(sPEx, ss - 1)
                    for c in range(NDC):
                        sy.dma_start(xnTb[sp][:, c * STR:(c + 1) * STR],
                                     xn_nat[ss * STR:(ss + 1) * STR, c * 128:(c + 1) * 128],
                                     transpose=True).then_inc(sTR[sp], 16)
                for sl2 in range(3):
                    sy.wait_ge(sXW[sl2], 16 * ((NTILE - sl2 + 2) // 3))
                pid = sy.partition_id()
                off = sy.compute_val(pid * OWN)
                sy.dma_start(xn_own_nat[:, :], xn_nat[bass.ds(off, OWN), :]).then_inc(sTOwn, 16)
                sy.wait_ge(sTOwn, 16)
                for c in range(NDC):
                    sy.dma_start(xnT_own[:, c * OWN:(c + 1) * OWN],
                                 xn_own_nat[:, c * 128:(c + 1) * 128],
                                 transpose=True).then_inc(sTOwn, 16)
                for pa in range(1, NP):
                    for ss in range(NSTR):
                        S2 = pa * NSTR + ss
                        sp = S2 % 2
                        sy.wait_ge(sPEx, S2 - 1)
                        for c in range(NDC):
                            sy.dma_start(xnTb[sp][:, c * STR:(c + 1) * STR],
                                         xn_nat[ss * STR:(ss + 1) * STR, c * 128:(c + 1) * 128],
                                         transpose=True).then_inc(sTR[sp], 16)

            @blkA.tensor
            def _(te):
                te.wait_ge(sINIT, 1)
                for pa in range(NP):
                    if pa >= 1:
                        te.wait_ge(sDVq, 16 * pa)
                    for s in range(NSTR):
                        S2 = pa * NSTR + s
                        sp = S2 % 2
                        te.wait_ge(sTR[sp], 256 * (S2 // 2 + 1))
                        for kb in range(HP):
                            if s == 0 and kb == 0:
                                te.wait_ge(sWk, 16 * (pa + 1))
                            if s == 0 and kb == 4:
                                te.wait_ge(sWkB, 16 * (pa + 1))
                            if kb < 2:
                                w = 16 * S2 - 11 + kb
                            elif kb < 4:
                                w = 16 * S2 - 3 + kb
                            else:
                                w = 16 * S2 + kb - 3
                            te.wait_ge(sDVkv, max(0, w))
                            for c in range(NDC):
                                ins = te.matmul(pC[kb % 4][:, :],
                                                wk_sb[:, c * 1024 + kb * 128: c * 1024 + (kb + 1) * 128],
                                                xnTb[sp][:, c * STR:(c + 1) * STR],
                                                start=(c == 0), stop=(c == NDC - 1))
                            ins.then_inc(sPEkvp, 1)
                        for tb in range(4):
                            for half in range(2):
                                hv = tb * 2 + half
                                if s == 0 and hv == 0:
                                    te.wait_ge(sWv, 16 * (pa + 1))
                                if s == 0 and hv == 1:
                                    te.wait_ge(sWvB, 16 * (pa + 1))
                                if hv < 2:
                                    w = 16 * S2 + 7 + hv
                                else:
                                    w = 16 * S2 + 7 + hv
                                te.wait_ge(sDVkv, max(0, w))
                                for c in range(NDC):
                                    ins = te.matmul(pC[2 + hv % 2][:, :],
                                                    xnTb[sp][:, c * STR + tb * 128: c * STR + (tb + 1) * 128],
                                                    wv_sb[:, c * 1024 + half * 512: c * 1024 + (half + 1) * 512],
                                                    start=(c == 0), stop=(c == NDC - 1))
                                if hv < HP - 1:
                                    ins.then_inc(sPEkvp, 1)
                                else:
                                    ins.then_inc(sPEx, 1)
                    if pa == 0:
                        te.wait_ge(sTOwn, 16 * (NDC + 1))
                    te.wait_ge(sDVkv, 16 * (NSTR * (pa + 1) - 1) + 6)
                    for qb in range(16):
                        gq = 16 * pa + qb
                        hrel = qb // 2
                        ist = qb % 2
                        if qb == 0:
                            te.wait_ge(sWq, 16 * (pa + 1))
                        if qb == 8:
                            te.wait_ge(sWqB, 16 * (pa + 1))
                        if qb >= 2:
                            te.wait_ge(sDVq, gq - 1)
                        for c in range(NDC):
                            ins = te.matmul(pC[qb % 2][:, :],
                                            wk_sb[:, c * 1024 + hrel * 128: c * 1024 + (hrel + 1) * 128],
                                            xnT_own[:, c * OWN + ist * STR: c * OWN + (ist + 1) * STR],
                                            start=(c == 0), stop=(c == NDC - 1))
                        ins.then_inc(sPEqs, 1)

        with ExitStack() as sb1:
            kT_sb = [sb1.enter_context(nc.sbuf_tensor(f"kT{i}", [128, NT], BF16)) for i in range(2)]
            vb = [sb1.enter_context(nc.sbuf_tensor(f"vb{i}", [128, 64 * VSLOT], BF16)) for i in range(2)]
            PT = [sb1.enter_context(nc.sbuf_tensor(f"PT{i}", [128, 512], BF16)) for i in range(2)]
            ctx_sb = [sb1.enter_context(nc.sbuf_tensor(f"ctx{i}", [128, 1024], BF16)) for i in range(2)]
            craw = [sb1.enter_context(nc.sbuf_tensor(f"craw{i}", [128, 1024], F32)) for i in range(2)]
            rec_sb = sb1.enter_context(nc.sbuf_tensor("rec_sb", [128, 4], F32))
            wg_sb = [sb1.enter_context(nc.sbuf_tensor(f"wg{i}", [128, 2048], BF16)) for i in range(2)]
            wu_sb = [sb1.enter_context(nc.sbuf_tensor(f"wu{i}", [128, 2048], BF16)) for i in range(2)]
            esb = [sb1.enter_context(nc.sbuf_tensor(f"esb{i}", [128, 512], F32)) for i in range(2)]
            rsb = [sb1.enter_context(nc.sbuf_tensor(f"rsb{i}", [128, 512], F32)) for i in range(2)]
            gated_sb = [sb1.enter_context(nc.sbuf_tensor(f"gat{i}", [128, 1024], BF16)) for i in range(2)]

            Sbuf = [pS[0][:, 0:512], pS[0][:, 512:1024]]
            ctxbank = [(pC[0], 0), (pC[0], 256), (pC[1], 0), (pC[1], 256)]
            Gb = [pS[1][:, 0:512], pS[1][:, 512:1024]]
            Ub = [pC[2][:, :], pC[3][:, :]]

            blkB1 = sb1.enter_context(nc.Block())

            A_PE = None

            @blkB1.gpsimd
            def _(g):
                g.wait_ge(sDVq, 16 * NP)
                g.wait_ge(sKW[0], 16 * (8 * NSTR * NP))
                g.wait_ge(sKW[1], 16 * (8 * NSTR * NP))
                for sl2 in range(3):
                    g.wait_ge(sXW[sl2], 16 * ((NTILE - sl2 + 2) // 3))
                g.wait_ge(sTOwn, 16 * (NDC + 1))
                for h in range(NH):
                    hp = h % 2
                    if h >= 2:
                        g.wait_ge(sPEpv, 2 * (h - 1))
                    g.dma_start(kT_sb[hp][:, :], kT_scr[h, :, :]).then_inc(sKL[hp], 16)
                    vsrc = v_scr[:, h, :].rearrange("(jc p) c -> p jc c", p=128)
                    vdst = vb[hp][:, :].rearrange("p (jc w) -> p jc w", w=VSLOT)[:, :, 0:128]
                    for part in range(8):
                        g.dma_start(vdst[:, part * 8:(part + 1) * 8, :],
                                    vsrc[:, part * 8:(part + 1) * 8, :]).then_inc(sVL[hp], 16)
                    for b in range(4):
                        hb = 4 * h + b
                        bp = hb % 2
                        if hb >= 2:
                            g.wait_ge(sPEg, 2 * hb - 2)
                        g.dma_start(wg_sb[bp][:, :], wgb[hb, :, :]).then_inc(sWgu[bp], 16)
                        g.dma_start(wu_sb[bp][:, :], wub[hb, :, :]).then_inc(sWgu[bp], 16)
                    g.dma_start(wdb[h // 4, 16 * (h % 4):16 * (h % 4 + 1), :, :],
                                wdq[h // 4, 16 * (h % 4):16 * (h % 4 + 1), :, :]).then_inc(sWC, 16)
                    if h >= 12:
                        g.dma_start(wpb[h - 12, :, :], wpq[h - 12, :, :]).then_inc(sWC, 16)

            @blkB1.vector
            def _(v):
                v.wait_ge(sPEqs, 16 * NP)
                for i in range(2):
                    ones_view = vb[i][:, :].rearrange("p (s w) -> p s w", w=VSLOT)[:, :, 128:129]
                    ins = v.memset(ones_view, 1.0)
                ins.then_inc(sINIT, 1)
                for h in range(NH):
                    for ist in range(2):
                        u = h * 2 + ist
                        cp2 = u % 2

                        def gate_mult(seg):
                            gsb = u * 4 + seg
                            sp2 = gsb % 2
                            hb = gsb // 2
                            half = gsb % 2
                            v.wait_ge(sSil, gsb + 1)
                            if half == 0 and hb >= 2:
                                v.wait_ge(sGW[hb % 2], 16 * (hb // 2))
                            v.tensor_scalar_add(rsb[sp2][:, :], esb[sp2][:, :], cst[:, 5:6])
                            v.drain()
                            v.reciprocal(esb[sp2][:, :], rsb[sp2][:, :])
                            v.drain()
                            v.tensor_tensor(rsb[sp2][:, :], Gb[sp2], esb[sp2][:, :], ALU.mult)
                            v.drain()
                            v.tensor_tensor(gated_sb[hb % 2][:, half * 512:(half + 1) * 512],
                                            rsb[sp2][:, :], Ub[sp2],
                                            ALU.mult).then_inc(sDVgu, 1)
                            v.drain()

                        for seg in range(3):
                            gate_mult(seg)
                        v.wait_ge(sPEpv, u + 1)
                        v.drain()
                        v.tensor_copy(craw[cp2][:, 0:512], pC[0][:, :]).then_inc(sDVctx, 1)
                        v.tensor_copy(craw[cp2][:, 512:1024], pC[1][:, :]).then_inc(sDVctx, 1)
                        v.drain()
                        gate_mult(3)
                        if h >= 2 and ist == 0:
                            v.wait_ge(sCTXW[h % 2], 16 * (h // 2))
                        for ib in range(4):
                            base = (ib // 2) * 512 + (ib % 2) * 256
                            v.reciprocal(rec_sb[:, 0:1], craw[cp2][:, base + 128:base + 129])
                            v.drain()
                            col = ist * 512 + ib * 128
                            v.tensor_scalar_mul(ctx_sb[h % 2][:, col:col + 128],
                                                craw[cp2][:, base:base + 128],
                                                rec_sb[:, 0:1]).then_inc(sDVN, 1)
                            v.drain()

            @blkB1.scalar
            def _(sc):
                sc.wait_ge(sINIT, 2)
                for h in range(NH):
                    for ist in range(2):
                        u = h * 2 + ist
                        for jc in range(64):
                            gc = u * 64 + jc
                            pp = jc % 2
                            sc.wait_ge(sS, gc + 1)
                            if gc >= 2:
                                if jc == 0:
                                    sc.wait_ge(sPVd, u * 63)
                                elif jc == 1:
                                    sc.wait_ge(sPEpv, u)
                                else:
                                    sc.wait_ge(sPVd, u * 63 + jc - 1)
                            sc.activation(PT[pp][:, :], Sbuf[pp], AF.Exp,
                                          scale=cst[:, 1:2], bias=cst[:, 2:3]).then_inc(sEXP[pp], 1)
                            if jc % 16 == 15:
                                gsb = u * 4 + jc // 16
                                sp2 = gsb % 2
                                sc.wait_ge(sPEg, gsb + 1)
                                if gsb >= 2:
                                    sc.wait_ge(sDVgu, gsb - 1)
                                sc.activation(esb[sp2][:, :], Gb[sp2], AF.Exp,
                                              scale=cst[:, 4:5]).then_inc(sSil, 1)

            @blkB1.sync
            def _(sy):
                for h in range(NH):
                    for b in range(4):
                        hb = 4 * h + b
                        sy.wait_ge(sDVgu, 2 * hb + 2)
                        if hb >= 2:
                            sy.wait_ge(sGW[hb % 2], 16 * (hb // 2))
                        sy.dma_start(gated_scr[hb, :, :],
                                     gated_sb[hb % 2][:, :]).then_inc(sGW[hb % 2], 16)
                    sy.wait_ge(sDVN, 4 * (2 * h + 2))
                    if h >= 2:
                        sy.wait_ge(sCTXW[h % 2], 16 * (h // 2))
                    src = ctx_sb[h % 2][:, :].rearrange("p (b c) -> p b c", c=128)
                    dst = ctx_nat[:, :].rearrange("(b p) (hh c) -> hh p b c", p=128, c=128)[h]
                    sy.dma_start(dst, src).then_inc(sCTXW[h % 2], 16)

            @blkB1.tensor
            def _(te):
                te.wait_ge(sINIT, 2)
                for h in range(NH):
                    hp = h % 2
                    te.wait_ge(sKL[hp], 16 * (h // 2 + 1))
                    te.wait_ge(sVL[hp], 128 * (h // 2 + 1))
                    for ist in range(2):
                        u = h * 2 + ist
                        qslice = qT[:, h * OWN + ist * STR: h * OWN + (ist + 1) * STR]

                        def s_mm(jc):
                            gc = u * 64 + jc
                            pp = jc % 2
                            if gc >= 2:
                                te.wait_ge(sEXP[pp], gc // 2)
                            te.matmul(Sbuf[pp],
                                      kT_sb[hp][:, jc * 128:(jc + 1) * 128],
                                      qslice, start=True, stop=True).then_inc(sS, 1)

                        s_mm(0)
                        for jc in range(64):
                            gc = u * 64 + jc
                            pp = jc % 2
                            seg = jc // 16
                            gsb = u * 4 + seg
                            sp2 = gsb % 2
                            hb = gsb // 2
                            half = gsb % 2
                            bp = hb % 2
                            if jc < 63:
                                s_mm(jc + 1)
                            if jc % 16 == 0:
                                te.wait_ge(sWgu[bp], 32 * (hb // 2 + 1))
                                if gsb >= 2:
                                    te.wait_ge(sSil, gsb - 1)
                                    te.wait_ge(sDVgu, gsb - 1)
                            for m in (2 * (jc % 16), 2 * (jc % 16) + 1):
                                if m < 16:
                                    te.matmul(Gb[sp2],
                                              wg_sb[bp][:, m * 128:(m + 1) * 128],
                                              xnT_own[:, m * OWN + half * 512: m * OWN + (half + 1) * 512],
                                              start=(m == 0), stop=(m == 15))
                                else:
                                    c = m - 16
                                    ins_g = te.matmul(Ub[sp2],
                                                      wu_sb[bp][:, c * 128:(c + 1) * 128],
                                                      xnT_own[:, c * OWN + half * 512: c * OWN + (half + 1) * 512],
                                                      start=(c == 0), stop=(c == 15))
                                    if c == 15:
                                        ins_g.then_inc(sPEg, 1)
                            if jc == 0:
                                te.wait_ge(sDVctx, 2 * u)
                            te.wait_ge(sEXP[pp], gc // 2 + 1)
                            for ib in range(4):
                                bank, boff = ctxbank[ib]
                                ins = te.matmul(bank[:, boff:boff + 129],
                                                PT[pp][:, ib * 128:(ib + 1) * 128],
                                                vb[hp][:, jc * VSLOT: jc * VSLOT + 129],
                                                start=(jc == 0 and boff == 0),
                                                stop=(jc == 63 and boff == 256))
                            if jc < 63:
                                ins.then_inc(sPVd, 1)
                            else:
                                ins.then_inc(sPEpv, 1)

        with ExitStack() as sb2:
            gatedin = [sb2.enter_context(nc.sbuf_tensor(f"gin{i}", [128, 4096], BF16)) for i in range(3)]
            wd_sb = [sb2.enter_context(nc.sbuf_tensor(f"wd{i}", [128, 2048], BF16)) for i in range(3)]
            wpq_sb = [sb2.enter_context(nc.sbuf_tensor(f"wpq{i}", [128, 8192], BF16)) for i in range(2)]
            ctxT_sb = sb2.enter_context(nc.sbuf_tensor("ctxT_sb", [128, 16 * OWN], BF16,
                                                       is_transpose_dest=True))
            xo_sb = [sb2.enter_context(nc.sbuf_tensor(f"xo{i}", [128, 512], F32)) for i in range(4)]
            out_sb = [sb2.enter_context(nc.sbuf_tensor(f"os{i}", [128, 512], F32)) for i in range(2)]
            bias_sb = sb2.enter_context(nc.sbuf_tensor("bias_sb", [128, D], F32))

            banks = [pS[0][:, 0:512], pS[0][:, 512:1024],
                     pS[1][:, 0:512], pS[1][:, 512:1024],
                     pC[0][:, :], pC[1][:, :], pC[2][:, :], pC[3][:, :]]

            blkB2 = sb2.enter_context(nc.Block())

            def b2_barrier(eng):
                eng.wait_ge(sPEg, 128)
                eng.wait_ge(sSil, 128)
                eng.wait_ge(sDVgu, 128)
                eng.wait_ge(sDVctx, 64)
                eng.wait_ge(sDVN, 128)
                eng.wait_ge(sEXP[0], 1024)
                eng.wait_ge(sEXP[1], 1024)
                eng.wait_ge(sCTXW[0], 16 * 8)
                eng.wait_ge(sCTXW[1], 16 * 8)
                eng.wait_ge(sGW[0], 16 * 32)
                eng.wait_ge(sGW[1], 16 * 32)

            @blkB2.gpsimd
            def _(g):
                b2_barrier(g)
                g.wait_ge(sWC, 16 * (3 * NP - 2) + 16 * 20)
                g.dma_start(bias_sb[:, :], biasb[:, :]).then_inc(sBB, 16)
                for q in range(4):
                    if q >= 2:
                        g.wait_ge(sPEob, 8 * (q - 1))
                    g.dma_start(wpq_sb[q % 2][:, :], wpb[q, :, :]).then_inc(sWp[q % 2], 16)
                    for grp in range(16):
                        n = 16 * q + grp
                        cp = n % 3
                        if n >= 3:
                            g.wait_ge(sPEhc, n - 2)
                        g.dma_start(wd_sb[cp][:, :].rearrange("p (s f) -> p s f", s=4),
                                    wdb[q, 4 * grp:4 * grp + 4, :, :].rearrange(
                                        "s p f -> p s f")).then_inc(sGin[cp], 16)

            @blkB2.sync
            def _(sy):
                b2_barrier(sy)
                for hc in range(NDC):
                    sy.dma_start(ctxT_sb[:, hc * OWN:(hc + 1) * OWN],
                                 ctx_nat[:, hc * 128:(hc + 1) * 128],
                                 transpose=True).then_inc(sTctx, 16)
                for q in range(4):
                    for grp in range(16):
                        n = 16 * q + grp
                        cp = n % 3
                        if n >= 3:
                            sy.wait_ge(sPEhc, n - 2)
                        sy.dma_start(gatedin[cp][:, :].rearrange("p (s f) -> p s f", s=4),
                                     gated_scr[4 * grp:4 * grp + 4, :, :].rearrange(
                                         "s p f -> p s f")).then_inc(sGR[cp], 16)
            @blkB2.scalar
            def _(sc):
                b2_barrier(sc)

                def xo_load(n):
                    q, tb = n // 8, n % 8
                    sc.dma_start(xo_sb[n % 4][:, :],
                                 x_own[tb * 128:(tb + 1) * 128, q * 512:(q + 1) * 512]
                                 ).then_inc(sXO[n % 4], 16)

                for n0 in range(4):
                    xo_load(n0)
                for n in range(32):
                    q, tb = n // 8, n % 8
                    sc.wait_ge(sDVo, n + 1)
                    if n >= 2:
                        sc.wait_ge(sOUT[tb % 2], 16 * (n // 2))
                    sc.dma_start(out[tb * 128:(tb + 1) * 128, q * 512:(q + 1) * 512],
                                 out_sb[tb % 2][:, :]).then_inc(sOUT[tb % 2], 16)
                    if n + 4 < 32:
                        xo_load(n + 4)
                sc.wait_ge(sOUT[0], 16 * 16)
                sc.wait_ge(sOUT[1], 16 * 16)

            @blkB2.tensor
            def _(te):
                for q in range(4):
                    for grp in range(16):
                        n = 16 * q + grp
                        cp = n % 3
                        te.wait_ge(sGin[cp], 16 * (n // 3 + 1))
                        te.wait_ge(sGR[cp], 16 * (n // 3 + 1))
                        for sub in range(4):
                            for tb in range(8):
                                if grp == 0 and sub == 0 and q >= 1:
                                    te.wait_ge(sDVo, 8 * (q - 1) + tb + 1)
                                ins = te.matmul(banks[tb],
                                                gatedin[cp][:, sub * 1024 + tb * 128: sub * 1024 + (tb + 1) * 128],
                                                wd_sb[cp][:, sub * 512:(sub + 1) * 512],
                                                start=(grp == 0 and sub == 0), stop=False)
                        ins.then_inc(sPEhc, 1)
                    te.wait_ge(sWp[q % 2], 16 * (q // 2 + 1))
                    if q == 0:
                        te.wait_ge(sTctx, 16 * NDC)
                    for hc2 in range(NDC):
                        for tb in range(8):
                            ins = te.matmul(banks[tb],
                                            ctxT_sb[:, hc2 * OWN + tb * 128: hc2 * OWN + (tb + 1) * 128],
                                            wpq_sb[q % 2][:, hc2 * 512:(hc2 + 1) * 512],
                                            start=False, stop=(hc2 == NDC - 1))
                            if hc2 == NDC - 1:
                                ins.then_inc(sPEob, 1)

            @blkB2.vector
            def _(v):
                v.wait_ge(sBB, 16)

                def tt1(n):
                    q, tb = n // 8, n % 8
                    tp = tb % 2
                    v.wait_ge(sXO[n % 4], 16 * (n // 4 + 1))
                    if n >= 2:
                        v.wait_ge(sOUT[tp], 16 * (n // 2))
                    v.tensor_tensor(out_sb[tp][:, :], xo_sb[n % 4][:, :],
                                    bias_sb[:, q * 512:(q + 1) * 512], ALU.add)
                    v.drain()

                tt1(0)
                tt1(1)
                for n in range(32):
                    q, tb = n // 8, n % 8
                    tp = tb % 2
                    v.wait_ge(sPEob, n + 1)
                    v.tensor_tensor(out_sb[tp][:, :], out_sb[tp][:, :],
                                    banks[tb], ALU.add).then_inc(sDVo, 1)
                    v.drain()
                    if n + 2 < 32:
                        tt1(n + 2)

    return nc


def kernel(L0_attn_query, L0_attn_key, L0_attn_value, L0_attn_Wqkv, L0_attn_Wproj,
           L0_attn_bproj, L0_ffn_w_gate, L0_ffn_w_up, L0_ffn_w_down):
    x = np.ascontiguousarray(np.asarray(L0_attn_query, dtype=np.float32).reshape(NT, D))
    Wqkv = np.asarray(L0_attn_Wqkv, dtype=np.float32)
    Wproj = np.asarray(L0_attn_Wproj, dtype=np.float32)
    bproj = np.asarray(L0_attn_bproj, dtype=np.float32)
    w_gate = np.asarray(L0_ffn_w_gate, dtype=np.float32)
    w_up = np.asarray(L0_ffn_w_up, dtype=np.float32)
    w_down = np.asarray(L0_ffn_w_down, dtype=np.float32)

    Wr = Wqkv.reshape(NH, 3, 128, D)
    wq = Wr[:, 0].reshape(NH * 128, D)
    wk = Wr[:, 1].reshape(NH * 128, D)
    wv = Wr[:, 2].reshape(NH * 128, D)

    def per_pass(w):
        return np.ascontiguousarray(
            w.T.reshape(16, 128, NP, HP * 128).transpose(2, 1, 0, 3)).reshape(NP, 128, HP * 2048)

    ins = dict(
        x=x,
        wqp=per_pass(wq), wkp=per_pass(wk), wvp=per_pass(wv),
        wgb=np.ascontiguousarray(
            w_gate.T.reshape(16, 128, 64, 128).transpose(2, 1, 0, 3)).reshape(64, 128, 2048),
        wub=np.ascontiguousarray(
            w_up.T.reshape(16, 128, 64, 128).transpose(2, 1, 0, 3)).reshape(64, 128, 2048),
        wdq=np.ascontiguousarray(w_down.T.reshape(64, 128, 4, 512).transpose(2, 0, 1, 3)),
        wpq=np.ascontiguousarray(
            Wproj.T.reshape(16, 128, 4, 512).transpose(2, 1, 0, 3)).reshape(4, 128, 8192),
        biasb=np.ascontiguousarray(np.broadcast_to(bproj[None, :], (128, D))),
    )

    nc = build_nc()
    in_maps = []
    for c in range(8):
        m = dict(ins)
        m["x_own"] = np.ascontiguousarray(x[c * OWN:(c + 1) * OWN])
        in_maps.append(m)
    res = run_bass_kernel_spmd(nc, in_maps, core_ids=list(range(8)))
    full = np.concatenate([res.results[c]["out"] for c in range(8)], axis=0)
    return full.reshape(1, NT, D)


# version 18
# speedup vs baseline: 1.2442x; 1.1266x over previous
import numpy as np
from contextlib import ExitStack

import concourse.bass as bass
import concourse.mybir as mybir
from concourse.bass_utils import run_bass_kernel_spmd

F32 = mybir.dt.float32
BF16 = mybir.dt.bfloat16
AF = mybir.ActivationFunctionType
AX = mybir.AxisListType
ALU = mybir.AluOpType

D = 2048
NT = 8192
OWN = 1024
NH = 16
NP = 2
HP = 8
NDC = 16
NSTR = 16
STR = 512
NTILE = 64
NHB = 64
VSLOT = 132

INVD = 1.0 / 2048.0
EPS = 1e-5
QKSCALE = 0.08838834764831845
CSHIFT = 4.0


def build_nc():
    nc = bass.Bass("TRN2", target_bir_lowering=False)

    x = nc.dram_tensor("x", [NT, D], F32, kind="ExternalInput")
    x_own = nc.dram_tensor("x_own", [OWN, D], F32, kind="ExternalInput")
    wqp = nc.dram_tensor("wqp", [NP, 128, HP * 2048], F32, kind="ExternalInput")
    wkp = nc.dram_tensor("wkp", [NP, 128, HP * 2048], F32, kind="ExternalInput")
    wvp = nc.dram_tensor("wvp", [NP, 128, HP * 2048], F32, kind="ExternalInput")
    wgb = nc.dram_tensor("wgb", [NHB, 128, 2048], F32, kind="ExternalInput")
    wub = nc.dram_tensor("wub", [NHB, 128, 2048], F32, kind="ExternalInput")
    wdq = nc.dram_tensor("wdq", [4, 64, 128, 512], F32, kind="ExternalInput")
    wpq = nc.dram_tensor("wpq", [4, 128, 8192], F32, kind="ExternalInput")
    biasb = nc.dram_tensor("biasb", [128, D], F32, kind="ExternalInput")
    out = nc.dram_tensor("out", [OWN, D], F32, kind="ExternalOutput")

    xn_nat = nc.dram_tensor("xn_nat", [NT, D], BF16, kind="Internal")
    wkb = nc.dram_tensor("wkb", [NP, 128, HP * 2048], BF16, kind="Internal")
    wvb = nc.dram_tensor("wvb", [NP, 128, HP * 2048], BF16, kind="Internal")
    wqb = nc.dram_tensor("wqb", [NP, 128, HP * 2048], BF16, kind="Internal")
    wdb = nc.dram_tensor("wdb", [4, 64, 128, 512], BF16, kind="Internal")
    wpb = nc.dram_tensor("wpb", [4, 128, 8192], BF16, kind="Internal")
    xn_own_nat = nc.dram_tensor("xn_own_nat", [OWN, D], BF16, kind="Internal")
    kT_scr = nc.dram_tensor("kT_scr", [NH, 128, NT], BF16, kind="Internal")
    v_scr = nc.dram_tensor("v_scr", [NT, NH, 128], BF16, kind="Internal")
    ctx_nat = nc.dram_tensor("ctx_nat", [OWN, D], BF16, kind="Internal")
    gated_scr = nc.dram_tensor("gated_scr", [NHB, 128, OWN], BF16, kind="Internal")

    st = ExitStack()
    with st:
        xnT_own = st.enter_context(nc.sbuf_tensor("xnT_own", [128, NDC * OWN], BF16))
        qT = st.enter_context(nc.sbuf_tensor("qT", [128, NH * OWN], BF16))
        cst = st.enter_context(nc.sbuf_tensor("cst", [128, 8], F32))
        pS = [st.enter_context(nc.psum_tensor(f"pS{i}", [128, 1024], F32)) for i in range(2)]
        pC = [st.enter_context(nc.psum_tensor(f"pC{i}", [128, 512], F32)) for i in range(4)]

        def sem(name):
            return st.enter_context(nc.semaphore(name))

        sXin = [sem("sXin0"), sem("sXin1")]
        sSq = sem("sSq")
        sVar = sem("sVar")
        sStd = sem("sStd")
        sNR = sem("sNR")
        sLN = sem("sLN")
        sXW = [sem(f"sXW{i}") for i in range(3)]
        sTOwn = sem("sTOwn")
        sTR = [sem("sTR0"), sem("sTR1")]
        sPEx = sem("sPEx")
        sWk = sem("sWk")
        sWv = sem("sWv")
        sWq = sem("sWq")
        sWkB = sem("sWkB")
        sWvB = sem("sWvB")
        sWqB = sem("sWqB")
        sPEkvp = sem("sPEkvp")
        sDVkv = sem("sDVkv")
        sKW = [sem("sKW0"), sem("sKW1")]
        sPEqs = sem("sPEqs")
        sDVq = sem("sDVq")
        sKL = [sem("sKL0"), sem("sKL1")]
        sVL = [sem("sVL0"), sem("sVL1")]
        sS = sem("sS")
        sEXP = [sem("sEXP0"), sem("sEXP1")]
        sPVd = sem("sPVd")
        sPEpv = sem("sPEpv")
        sDVctx = sem("sDVctx")
        sDVN = sem("sDVN")
        sCTXW = [sem("sCTXW0"), sem("sCTXW1")]
        sWgu = [sem("sWgu0"), sem("sWgu1")]
        sPEg = sem("sPEg")
        sSil = sem("sSil")
        sDVgu = sem("sDVgu")
        sGW = [sem("sGW0"), sem("sGW1")]
        sTctx = sem("sTctx")
        sTctx2 = sem("sTctx2")
        sWp = [sem("sWp0"), sem("sWp1")]
        sGin = [sem(f"sGin{i}") for i in range(3)]
        sGR = [sem(f"sGR{i}") for i in range(3)]
        sPEhc = sem("sPEhc")
        sPEob = sem("sPEob")
        sXO = [sem(f"sXO{i}") for i in range(4)]
        sDVo = sem("sDVo")
        sOUT = [sem("sOUT0"), sem("sOUT1")]
        sBB = sem("sBB")
        sINIT = sem("sINIT")
        sWC = sem("sWC")

        with ExitStack() as sa:
            xinp = [sa.enter_context(nc.sbuf_tensor(f"xinp{i}", [128, 2 * D], BF16)) for i in range(2)]
            stats = [sa.enter_context(nc.sbuf_tensor(f"stats{i}", [128, 16], F32)) for i in range(4)]
            xn_bf = [sa.enter_context(nc.sbuf_tensor(f"xnbf{i}", [128, D], BF16)) for i in range(3)]
            xnTb = [sa.enter_context(nc.sbuf_tensor(f"xnTb{i}", [128, NDC * STR], BF16,
                                                    is_transpose_dest=True)) for i in range(2)]
            wk_sb = sa.enter_context(nc.sbuf_tensor("wk_sb", [128, HP * 2048], BF16))
            wv_sb = sa.enter_context(nc.sbuf_tensor("wv_sb", [128, HP * 2048], BF16))
            kst = [sa.enter_context(nc.sbuf_tensor(f"kst{i}", [128, 512], BF16)) for i in range(2)]
            vst = kst

            blkA = sa.enter_context(nc.Block())

            def whalf(ap, hf):
                return ap.rearrange("p (c w) -> p c w", w=1024)[:, :, hf * 512:(hf + 1) * 512]

            @blkA.gpsimd
            def _(g):
                def x_pair(j):
                    jp = j % 2
                    if j >= 2:
                        g.wait_ge(sLN, 2 * j - 2)
                    g.dma_start(xinp[jp][:, :].rearrange("p (s f) -> p s f", s=2),
                                x[j * 256:(j + 1) * 256, :].rearrange("(s p) f -> p s f", p=128)
                                ).then_inc(sXin[jp], 16)

                x_pair(0)
                x_pair(1)
                g.dma_start(whalf(wk_sb[:, :], 0), whalf(wkp[0], 0)).then_inc(sWk, 16)
                x_pair(2)
                g.dma_start(whalf(wk_sb[:, :], 1), whalf(wkp[0], 1)).then_inc(sWkB, 16)
                x_pair(3)
                g.dma_start(whalf(wv_sb[:, :], 0), whalf(wvp[0], 0)).then_inc(sWv, 16)
                x_pair(4)
                g.dma_start(whalf(wv_sb[:, :], 1), whalf(wvp[0], 1)).then_inc(sWvB, 16)
                for j in range(5, NTILE // 2):
                    x_pair(j)
                g.dma_start(wqb[0, :, :], wqp[0, :, :]).then_inc(sWC, 16)
                for pa in range(1, NP):
                    g.dma_start(wkb[pa, :, :], wkp[pa, :, :]).then_inc(sWC, 16)
                    g.dma_start(wvb[pa, :, :], wvp[pa, :, :]).then_inc(sWC, 16)
                    g.dma_start(wqb[pa, :, :], wqp[pa, :, :]).then_inc(sWC, 16)
                g.wait_ge(sWC, 16 * (3 * NP - 2))
                for pa in range(NP):
                    g.wait_ge(sPEkvp, 15 * (NSTR * (pa + 1) - 1) + 8)
                    g.dma_start(whalf(wk_sb[:, :], 0), whalf(wqb[pa], 0)).then_inc(sWq, 16)
                    g.dma_start(whalf(wk_sb[:, :], 1), whalf(wqb[pa], 1)).then_inc(sWqB, 16)
                    if pa + 1 < NP:
                        g.wait_ge(sPEx, NSTR * (pa + 1))
                        g.dma_start(whalf(wv_sb[:, :], 0), whalf(wvb[pa + 1], 0)).then_inc(sWv, 16)
                        g.wait_ge(sPEqs, 16 * pa + 8)
                        g.dma_start(whalf(wk_sb[:, :], 0), whalf(wkb[pa + 1], 0)).then_inc(sWk, 16)
                        g.dma_start(whalf(wv_sb[:, :], 1), whalf(wvb[pa + 1], 1)).then_inc(sWvB, 16)
                        g.wait_ge(sPEqs, 16 * (pa + 1))
                        g.dma_start(whalf(wk_sb[:, :], 1), whalf(wkb[pa + 1], 1)).then_inc(sWkB, 16)

            @blkA.vector
            def _(v):
                v.memset(cst[:, 0:1], EPS)
                v.memset(cst[:, 1:2], QKSCALE)
                v.memset(cst[:, 2:3], -CSHIFT)
                v.memset(cst[:, 3:4], INVD)
                v.memset(cst[:, 4:5], -1.0)
                v.memset(cst[:, 5:6], 1.0).then_inc(sINIT, 1)

                def ln_tile(t):
                    sl = t % 4
                    S = stats[sl]
                    xv = xinp[(t // 2) % 2][:, (t % 2) * D:(t % 2 + 1) * D]
                    v.wait_ge(sXin[(t // 2) % 2], 16 * (t // 4 + 1))
                    v.reduce_sum(S[:, 0:1], xv, axis=AX.X)
                    v.drain()
                    v.wait_ge(sSq, t + 1)
                    v.tensor_scalar_mul(S[:, 2:3], S[:, 0:1], cst[:, 3:4])
                    v.drain()
                    v.tensor_mul(S[:, 3:4], S[:, 2:3], S[:, 2:3])
                    v.drain()
                    v.scalar_tensor_tensor(S[:, 4:5], S[:, 1:2], cst[:, 3:4], S[:, 3:4],
                                           ALU.mult, ALU.subtract).then_inc(sVar, 1)
                    v.wait_ge(sStd, t + 1)
                    v.reciprocal(S[:, 6:7], S[:, 5:6])
                    v.drain()
                    v.scalar_tensor_tensor(S[:, 7:8], S[:, 2:3], cst[:, 4:5], S[:, 6:7],
                                           ALU.mult, ALU.mult).then_inc(sNR, 1)
                    v.drain()
                    v.tensor_scalar(xn_bf[t % 3][:, :], xv, S[:, 6:7], S[:, 7:8],
                                    ALU.mult, ALU.add).then_inc(sLN, 1)

                LN_SLOTS = {1: 0, 2: 1, 5: 2, 8: 3}

                def kv_evac_stripe(pa, s, lns=()):
                    S2 = pa * NSTR + s
                    for i in range(2 * HP):
                        if i in LN_SLOTS and LN_SLOTS[i] < len(lns):
                            ln_tile(lns[LN_SLOTS[i]])
                        n = 16 * S2 + i
                        if i < HP:
                            kb = i
                            v.wait_ge(sPEkvp, 15 * S2 + kb + 1)
                            if n >= 2:
                                v.wait_ge(sKW[n % 2], 16 * (n // 2))
                            v.tensor_copy(kst[n % 2][:, :], pC[kb % 4][:, :]).then_inc(sDVkv, 1)
                        else:
                            hv = i - HP
                            if hv < HP - 1:
                                v.wait_ge(sPEkvp, 15 * S2 + HP + hv + 1)
                            else:
                                v.wait_ge(sPEx, S2 + 1)
                            v.wait_ge(sKW[n % 2], 16 * (n // 2))
                            v.tensor_copy(vst[n % 2][:, :], pC[2 + hv % 2][:, :]).then_inc(sDVkv, 1)

                def q_evac(pa):
                    for qb in range(16):
                        gq = 16 * pa + qb
                        v.wait_ge(sPEqs, gq + 1)
                        h = pa * HP + qb // 2
                        ist = qb % 2
                        v.tensor_copy(qT[:, h * OWN + ist * STR: h * OWN + (ist + 1) * STR],
                                      pC[qb % 2][:, :]).then_inc(sDVq, 1)

                LOOK = 12
                for tt in range(LOOK):
                    ln_tile(tt)
                for s in range(NSTR):
                    kv_evac_stripe(0, s,
                                   list(range(4 * s + LOOK, min(4 * s + LOOK + 4, NTILE))))
                q_evac(0)
                for pa in range(1, NP):
                    for s in range(NSTR):
                        kv_evac_stripe(pa, s)
                    q_evac(pa)

            @blkA.scalar
            def _(sc):
                sc.wait_ge(sINIT, 1)

                def ln_acts(t):
                    sl = t % 4
                    S = stats[sl]
                    xv = xinp[(t // 2) % 2][:, (t % 2) * D:(t % 2 + 1) * D]
                    sc.wait_ge(sXin[(t // 2) % 2], 16 * (t // 4 + 1))
                    if t >= 4:
                        sc.wait_ge(sLN, t - 3)
                    if t >= 3:
                        sc.wait_ge(sLN, t - 2)
                        sc.wait_ge(sXW[t % 3], 16 * (t // 3))
                    sc.drain()
                    sc.activation(xn_bf[t % 3][:, :], xv, AF.Square,
                                  accum_out=S[:, 1:2]).then_inc(sSq, 1)
                    sc.wait_ge(sVar, t + 1)
                    sc.activation(S[:, 5:6], S[:, 4:5], AF.Sqrt,
                                  bias=cst[:, 0:1]).then_inc(sStd, 1)

                LN_SLOTS = {1: 0, 2: 1, 5: 2, 8: 3}

                def kv_write_stripe(pa, s, lns=()):
                    S2 = pa * NSTR + s
                    for i in range(2 * HP):
                        if i in LN_SLOTS and LN_SLOTS[i] < len(lns):
                            ln_acts(lns[LN_SLOTS[i]])
                        n = 16 * S2 + i
                        sc.wait_ge(sDVkv, n + 1)
                        if n >= 2:
                            sc.wait_ge(sKW[n % 2], 16 * (n // 2))
                        if i < HP:
                            h = pa * HP + i
                            sc.dma_start(kT_scr[h, :, s * STR:(s + 1) * STR],
                                         kst[n % 2][:, :]).then_inc(sKW[n % 2], 16)
                        else:
                            hv = i - HP
                            tb, half = hv // 2, hv % 2
                            dst = v_scr[s * STR + tb * 128: s * STR + (tb + 1) * 128,
                                        pa * HP + half * 4: pa * HP + (half + 1) * 4, :]
                            src = vst[n % 2][:, :].rearrange("p (h c) -> p h c", c=128)
                            sc.dma_start(dst, src).then_inc(sKW[n % 2], 16)

                LOOK = 12
                for tt in range(LOOK):
                    ln_acts(tt)
                for s in range(NSTR):
                    kv_write_stripe(0, s,
                                    list(range(4 * s + LOOK, min(4 * s + LOOK + 4, NTILE))))
                for pa in range(1, NP):
                    for s in range(NSTR):
                        kv_write_stripe(pa, s)

            @blkA.sync
            def _(sy):
                for t in range(NTILE):
                    sl = t % 3
                    sy.wait_ge(sLN, t + 1)
                    if t >= 3:
                        sy.wait_ge(sXW[sl], 16 * (t // 3))
                    sy.dma_start(xn_nat[t * 128:(t + 1) * 128, :], xn_bf[sl][:, :]).then_inc(sXW[sl], 16)
                    if t % 4 == 3 and t >= 4:
                        ss = t // 4 - 1
                        sp = ss % 2
                        for sl2 in range(3):
                            sy.wait_ge(sXW[sl2], 16 * ((4 * ss + 6 - sl2) // 3))
                        if ss >= 2:
                            sy.wait_ge(sPEx, ss - 1)
                        for c in range(NDC):
                            sy.dma_start(xnTb[sp][:, c * STR:(c + 1) * STR],
                                         xn_nat[ss * STR:(ss + 1) * STR, c * 128:(c + 1) * 128],
                                         transpose=True).then_inc(sTR[sp], 16)
                if True:
                    ss = NSTR - 1
                    sp = ss % 2
                    for sl2 in range(3):
                        sy.wait_ge(sXW[sl2], 16 * ((4 * ss + 6 - sl2) // 3))
                    sy.wait_ge(sPEx, ss - 1)
                    for c in range(NDC):
                        sy.dma_start(xnTb[sp][:, c * STR:(c + 1) * STR],
                                     xn_nat[ss * STR:(ss + 1) * STR, c * 128:(c + 1) * 128],
                                     transpose=True).then_inc(sTR[sp], 16)
                for sl2 in range(3):
                    sy.wait_ge(sXW[sl2], 16 * ((NTILE - sl2 + 2) // 3))
                pid = sy.partition_id()
                off = sy.compute_val(pid * OWN)
                sy.dma_start(xn_own_nat[:, :], xn_nat[bass.ds(off, OWN), :]).then_inc(sTOwn, 16)
                sy.wait_ge(sTOwn, 16)
                for c in range(NDC):
                    sy.dma_start(xnT_own[:, c * OWN:(c + 1) * OWN],
                                 xn_own_nat[:, c * 128:(c + 1) * 128],
                                 transpose=True).then_inc(sTOwn, 16)
                for pa in range(1, NP):
                    for ss in range(NSTR):
                        S2 = pa * NSTR + ss
                        sp = S2 % 2
                        sy.wait_ge(sPEx, S2 - 1)
                        for c in range(NDC):
                            sy.dma_start(xnTb[sp][:, c * STR:(c + 1) * STR],
                                         xn_nat[ss * STR:(ss + 1) * STR, c * 128:(c + 1) * 128],
                                         transpose=True).then_inc(sTR[sp], 16)

            @blkA.tensor
            def _(te):
                te.wait_ge(sINIT, 1)
                for pa in range(NP):
                    if pa >= 1:
                        te.wait_ge(sDVq, 16 * pa)
                    for s in range(NSTR):
                        S2 = pa * NSTR + s
                        sp = S2 % 2
                        te.wait_ge(sTR[sp], 256 * (S2 // 2 + 1))
                        for kb in range(HP):
                            if s == 0 and kb == 0:
                                te.wait_ge(sWk, 16 * (pa + 1))
                            if s == 0 and kb == 4:
                                te.wait_ge(sWkB, 16 * (pa + 1))
                            if kb < 2:
                                w = 16 * S2 - 11 + kb
                            elif kb < 4:
                                w = 16 * S2 - 3 + kb
                            else:
                                w = 16 * S2 + kb - 3
                            te.wait_ge(sDVkv, max(0, w))
                            for c in range(NDC):
                                ins = te.matmul(pC[kb % 4][:, :],
                                                wk_sb[:, c * 1024 + kb * 128: c * 1024 + (kb + 1) * 128],
                                                xnTb[sp][:, c * STR:(c + 1) * STR],
                                                start=(c == 0), stop=(c == NDC - 1))
                            ins.then_inc(sPEkvp, 1)
                        for tb in range(4):
                            for half in range(2):
                                hv = tb * 2 + half
                                if s == 0 and hv == 0:
                                    te.wait_ge(sWv, 16 * (pa + 1))
                                if s == 0 and hv == 1:
                                    te.wait_ge(sWvB, 16 * (pa + 1))
                                if hv < 2:
                                    w = 16 * S2 + 7 + hv
                                else:
                                    w = 16 * S2 + 7 + hv
                                te.wait_ge(sDVkv, max(0, w))
                                for c in range(NDC):
                                    ins = te.matmul(pC[2 + hv % 2][:, :],
                                                    xnTb[sp][:, c * STR + tb * 128: c * STR + (tb + 1) * 128],
                                                    wv_sb[:, c * 1024 + half * 512: c * 1024 + (half + 1) * 512],
                                                    start=(c == 0), stop=(c == NDC - 1))
                                if hv < HP - 1:
                                    ins.then_inc(sPEkvp, 1)
                                else:
                                    ins.then_inc(sPEx, 1)
                    if pa == 0:
                        te.wait_ge(sTOwn, 16 * (NDC + 1))
                    te.wait_ge(sDVkv, 16 * (NSTR * (pa + 1) - 1) + 6)
                    for qb in range(16):
                        gq = 16 * pa + qb
                        hrel = qb // 2
                        ist = qb % 2
                        if qb == 0:
                            te.wait_ge(sWq, 16 * (pa + 1))
                        if qb == 8:
                            te.wait_ge(sWqB, 16 * (pa + 1))
                        if qb >= 2:
                            te.wait_ge(sDVq, gq - 1)
                        for c in range(NDC):
                            ins = te.matmul(pC[qb % 2][:, :],
                                            wk_sb[:, c * 1024 + hrel * 128: c * 1024 + (hrel + 1) * 128],
                                            xnT_own[:, c * OWN + ist * STR: c * OWN + (ist + 1) * STR],
                                            start=(c == 0), stop=(c == NDC - 1))
                        ins.then_inc(sPEqs, 1)

        ctxT_sb = st.enter_context(nc.sbuf_tensor("ctxT_sb", [128, 16 * OWN], BF16,
                                                  is_transpose_dest=True))

        with ExitStack() as sb1:
            kT_sb = [sb1.enter_context(nc.sbuf_tensor(f"kT{i}", [128, NT], BF16)) for i in range(2)]
            vb = [sb1.enter_context(nc.sbuf_tensor(f"vb{i}", [128, 64 * VSLOT], BF16)) for i in range(2)]
            PT = [sb1.enter_context(nc.sbuf_tensor(f"PT{i}", [128, 512], BF16)) for i in range(2)]
            ctx_sb = [sb1.enter_context(nc.sbuf_tensor(f"ctx{i}", [128, 1024], BF16)) for i in range(2)]
            craw = [sb1.enter_context(nc.sbuf_tensor(f"craw{i}", [128, 1024], F32)) for i in range(2)]
            rec_sb = sb1.enter_context(nc.sbuf_tensor("rec_sb", [128, 4], F32))
            wg_sb = [sb1.enter_context(nc.sbuf_tensor(f"wg{i}", [128, 2048], BF16)) for i in range(2)]
            wu_sb = [sb1.enter_context(nc.sbuf_tensor(f"wu{i}", [128, 2048], BF16)) for i in range(2)]
            esb = [sb1.enter_context(nc.sbuf_tensor(f"esb{i}", [128, 512], F32)) for i in range(2)]
            rsb = [sb1.enter_context(nc.sbuf_tensor(f"rsb{i}", [128, 512], F32)) for i in range(2)]
            gated_sb = [sb1.enter_context(nc.sbuf_tensor(f"gat{i}", [128, 1024], BF16)) for i in range(2)]

            Sbuf = [pS[0][:, 0:512], pS[0][:, 512:1024]]
            ctxbank = [(pC[0], 0), (pC[0], 256), (pC[1], 0), (pC[1], 256)]
            Gb = [pS[1][:, 0:512], pS[1][:, 512:1024]]
            Ub = [pC[2][:, :], pC[3][:, :]]

            blkB1 = sb1.enter_context(nc.Block())

            A_PE = None

            @blkB1.gpsimd
            def _(g):
                g.wait_ge(sDVq, 16 * NP)
                g.wait_ge(sKW[0], 16 * (8 * NSTR * NP))
                g.wait_ge(sKW[1], 16 * (8 * NSTR * NP))
                for sl2 in range(3):
                    g.wait_ge(sXW[sl2], 16 * ((NTILE - sl2 + 2) // 3))
                g.wait_ge(sTOwn, 16 * (NDC + 1))
                for h in range(NH):
                    hp = h % 2
                    if h >= 2:
                        g.wait_ge(sPEpv, 2 * (h - 1))
                    g.dma_start(kT_sb[hp][:, :], kT_scr[h, :, :]).then_inc(sKL[hp], 16)
                    vsrc = v_scr[:, h, :].rearrange("(jc p) c -> p jc c", p=128)
                    vdst = vb[hp][:, :].rearrange("p (jc w) -> p jc w", w=VSLOT)[:, :, 0:128]
                    for part in range(8):
                        g.dma_start(vdst[:, part * 8:(part + 1) * 8, :],
                                    vsrc[:, part * 8:(part + 1) * 8, :]).then_inc(sVL[hp], 16)
                    for b in range(4):
                        hb = 4 * h + b
                        bp = hb % 2
                        if hb >= 2:
                            g.wait_ge(sPEg, 2 * hb - 2)
                        g.dma_start(wg_sb[bp][:, :], wgb[hb, :, :]).then_inc(sWgu[bp], 16)
                        g.dma_start(wu_sb[bp][:, :], wub[hb, :, :]).then_inc(sWgu[bp], 16)
                    g.dma_start(wdb[h // 4, 16 * (h % 4):16 * (h % 4 + 1), :, :],
                                wdq[h // 4, 16 * (h % 4):16 * (h % 4 + 1), :, :]).then_inc(sWC, 16)
                    if h >= 12:
                        g.dma_start(wpb[h - 12, :, :], wpq[h - 12, :, :]).then_inc(sWC, 16)

            @blkB1.vector
            def _(v):
                v.wait_ge(sPEqs, 16 * NP)
                for i in range(2):
                    ones_view = vb[i][:, :].rearrange("p (s w) -> p s w", w=VSLOT)[:, :, 128:129]
                    ins = v.memset(ones_view, 1.0)
                ins.then_inc(sINIT, 1)
                for h in range(NH):
                    for ist in range(2):
                        u = h * 2 + ist
                        cp2 = u % 2

                        def gate_mult(seg):
                            gsb = u * 4 + seg
                            sp2 = gsb % 2
                            hb = gsb // 2
                            half = gsb % 2
                            v.wait_ge(sSil, gsb + 1)
                            if half == 0 and hb >= 2:
                                v.wait_ge(sGW[hb % 2], 16 * (hb // 2))
                            v.tensor_scalar_add(rsb[sp2][:, :], esb[sp2][:, :], cst[:, 5:6])
                            v.drain()
                            v.reciprocal(esb[sp2][:, :], rsb[sp2][:, :])
                            v.drain()
                            v.tensor_tensor(rsb[sp2][:, :], Gb[sp2], esb[sp2][:, :], ALU.mult)
                            v.drain()
                            v.tensor_tensor(gated_sb[hb % 2][:, half * 512:(half + 1) * 512],
                                            rsb[sp2][:, :], Ub[sp2],
                                            ALU.mult).then_inc(sDVgu, 1)
                            v.drain()

                        for seg in range(3):
                            gate_mult(seg)
                        v.wait_ge(sPEpv, u + 1)
                        v.drain()
                        v.tensor_copy(craw[cp2][:, 0:512], pC[0][:, :]).then_inc(sDVctx, 1)
                        v.tensor_copy(craw[cp2][:, 512:1024], pC[1][:, :]).then_inc(sDVctx, 1)
                        v.drain()
                        gate_mult(3)
                        if h >= 2 and ist == 0:
                            v.wait_ge(sCTXW[h % 2], 16 * (h // 2))
                        for ib in range(4):
                            base = (ib // 2) * 512 + (ib % 2) * 256
                            v.reciprocal(rec_sb[:, 0:1], craw[cp2][:, base + 128:base + 129])
                            v.drain()
                            col = ist * 512 + ib * 128
                            v.tensor_scalar_mul(ctx_sb[h % 2][:, col:col + 128],
                                                craw[cp2][:, base:base + 128],
                                                rec_sb[:, 0:1]).then_inc(sDVN, 1)
                            v.drain()

            @blkB1.scalar
            def _(sc):
                sc.wait_ge(sINIT, 2)
                for h in range(NH):
                    for ist in range(2):
                        u = h * 2 + ist
                        for jc in range(64):
                            gc = u * 64 + jc
                            pp = jc % 2
                            sc.wait_ge(sS, gc + 1)
                            if gc >= 2:
                                if jc == 0:
                                    sc.wait_ge(sPVd, u * 63)
                                elif jc == 1:
                                    sc.wait_ge(sPEpv, u)
                                else:
                                    sc.wait_ge(sPVd, u * 63 + jc - 1)
                            sc.activation(PT[pp][:, :], Sbuf[pp], AF.Exp,
                                          scale=cst[:, 1:2], bias=cst[:, 2:3]).then_inc(sEXP[pp], 1)
                            if jc % 16 == 15:
                                gsb = u * 4 + jc // 16
                                sp2 = gsb % 2
                                sc.wait_ge(sPEg, gsb + 1)
                                if gsb >= 2:
                                    sc.wait_ge(sDVgu, gsb - 1)
                                sc.activation(esb[sp2][:, :], Gb[sp2], AF.Exp,
                                              scale=cst[:, 4:5]).then_inc(sSil, 1)

            @blkB1.sync
            def _(sy):
                for h in range(NH):
                    for b in range(4):
                        hb = 4 * h + b
                        sy.wait_ge(sDVgu, 2 * hb + 2)
                        if hb >= 2:
                            sy.wait_ge(sGW[hb % 2], 16 * (hb // 2))
                        sy.dma_start(gated_scr[hb, :, :],
                                     gated_sb[hb % 2][:, :]).then_inc(sGW[hb % 2], 16)
                    sy.wait_ge(sDVN, 4 * (2 * h + 2))
                    if h >= 2:
                        sy.wait_ge(sCTXW[h % 2], 16 * (h // 2))
                    src = ctx_sb[h % 2][:, :].rearrange("p (b c) -> p b c", c=128)
                    dst = ctx_nat[:, :].rearrange("(b p) (hh c) -> hh p b c", p=128, c=128)[h]
                    sy.dma_start(dst, src).then_inc(sCTXW[h % 2], 16)
                    sy.wait_ge(sCTXW[h % 2], 16 * (h // 2 + 1))
                    sy.dma_start(ctxT_sb[:, h * OWN:(h + 1) * OWN],
                                 ctx_nat[:, h * 128:(h + 1) * 128],
                                 transpose=True).then_inc(sTctx, 16)

            @blkB1.tensor
            def _(te):
                te.wait_ge(sINIT, 2)
                for h in range(NH):
                    hp = h % 2
                    te.wait_ge(sKL[hp], 16 * (h // 2 + 1))
                    te.wait_ge(sVL[hp], 128 * (h // 2 + 1))
                    for ist in range(2):
                        u = h * 2 + ist
                        qslice = qT[:, h * OWN + ist * STR: h * OWN + (ist + 1) * STR]

                        def s_mm(jc):
                            gc = u * 64 + jc
                            pp = jc % 2
                            if gc >= 2:
                                te.wait_ge(sEXP[pp], gc // 2)
                            te.matmul(Sbuf[pp],
                                      kT_sb[hp][:, jc * 128:(jc + 1) * 128],
                                      qslice, start=True, stop=True).then_inc(sS, 1)

                        s_mm(0)
                        for jc in range(64):
                            gc = u * 64 + jc
                            pp = jc % 2
                            seg = jc // 16
                            gsb = u * 4 + seg
                            sp2 = gsb % 2
                            hb = gsb // 2
                            half = gsb % 2
                            bp = hb % 2
                            if jc < 63:
                                s_mm(jc + 1)
                            if jc % 16 == 0:
                                te.wait_ge(sWgu[bp], 32 * (hb // 2 + 1))
                                if gsb >= 2:
                                    te.wait_ge(sSil, gsb - 1)
                                    te.wait_ge(sDVgu, gsb - 1)
                            for m in (2 * (jc % 16), 2 * (jc % 16) + 1):
                                if m < 16:
                                    te.matmul(Gb[sp2],
                                              wg_sb[bp][:, m * 128:(m + 1) * 128],
                                              xnT_own[:, m * OWN + half * 512: m * OWN + (half + 1) * 512],
                                              start=(m == 0), stop=(m == 15))
                                else:
                                    c = m - 16
                                    ins_g = te.matmul(Ub[sp2],
                                                      wu_sb[bp][:, c * 128:(c + 1) * 128],
                                                      xnT_own[:, c * OWN + half * 512: c * OWN + (half + 1) * 512],
                                                      start=(c == 0), stop=(c == 15))
                                    if c == 15:
                                        ins_g.then_inc(sPEg, 1)
                            if jc == 0:
                                te.wait_ge(sDVctx, 2 * u)
                            te.wait_ge(sEXP[pp], gc // 2 + 1)
                            for ib in range(4):
                                bank, boff = ctxbank[ib]
                                ins = te.matmul(bank[:, boff:boff + 129],
                                                PT[pp][:, ib * 128:(ib + 1) * 128],
                                                vb[hp][:, jc * VSLOT: jc * VSLOT + 129],
                                                start=(jc == 0 and boff == 0),
                                                stop=(jc == 63 and boff == 256))
                            if jc < 63:
                                ins.then_inc(sPVd, 1)
                            else:
                                ins.then_inc(sPEpv, 1)

        with ExitStack() as sb2:
            gatedin = [sb2.enter_context(nc.sbuf_tensor(f"gin{i}", [128, 4096], BF16)) for i in range(3)]
            wd_sb = [sb2.enter_context(nc.sbuf_tensor(f"wd{i}", [128, 2048], BF16)) for i in range(3)]
            wpq_sb = [sb2.enter_context(nc.sbuf_tensor(f"wpq{i}", [128, 8192], BF16)) for i in range(2)]
            xo_sb = [sb2.enter_context(nc.sbuf_tensor(f"xo{i}", [128, 512], F32)) for i in range(4)]
            out_sb = [sb2.enter_context(nc.sbuf_tensor(f"os{i}", [128, 512], F32)) for i in range(2)]
            bias_sb = sb2.enter_context(nc.sbuf_tensor("bias_sb", [128, D], F32))

            banks = [pS[0][:, 0:512], pS[0][:, 512:1024],
                     pS[1][:, 0:512], pS[1][:, 512:1024],
                     pC[0][:, :], pC[1][:, :], pC[2][:, :], pC[3][:, :]]

            blkB2 = sb2.enter_context(nc.Block())

            def b2_barrier(eng):
                eng.wait_ge(sPEg, 128)
                eng.wait_ge(sSil, 128)
                eng.wait_ge(sDVgu, 128)
                eng.wait_ge(sDVctx, 64)
                eng.wait_ge(sDVN, 128)
                eng.wait_ge(sEXP[0], 1024)
                eng.wait_ge(sEXP[1], 1024)
                eng.wait_ge(sCTXW[0], 16 * 8)
                eng.wait_ge(sCTXW[1], 16 * 8)
                eng.wait_ge(sGW[0], 16 * 32)
                eng.wait_ge(sGW[1], 16 * 32)

            @blkB2.gpsimd
            def _(g):
                b2_barrier(g)
                g.wait_ge(sWC, 16 * (3 * NP - 2) + 16 * 20)
                g.dma_start(bias_sb[:, :], biasb[:, :]).then_inc(sBB, 16)
                for q in range(4):
                    for grp in range(16):
                        n = 16 * q + grp
                        cp = n % 3
                        if n >= 3:
                            g.wait_ge(sPEhc, n - 2)
                        g.dma_start(wd_sb[cp][:, :].rearrange("p (s f) -> p s f", s=4),
                                    wdb[q, 4 * grp:4 * grp + 4, :, :].rearrange(
                                        "s p f -> p s f")).then_inc(sGin[cp], 16)
                        if grp == 3:
                            if q >= 2:
                                g.wait_ge(sPEob, 8 * (q - 1))
                            g.dma_start(wpq_sb[q % 2][:, :], wpb[q, :, :]).then_inc(sWp[q % 2], 16)

            @blkB2.sync
            def _(sy):
                b2_barrier(sy)
                for q in range(4):
                    for grp in range(16):
                        n = 16 * q + grp
                        cp = n % 3
                        if n >= 3:
                            sy.wait_ge(sPEhc, n - 2)
                        sy.dma_start(gatedin[cp][:, :].rearrange("p (s f) -> p s f", s=4),
                                     gated_scr[4 * grp:4 * grp + 4, :, :].rearrange(
                                         "s p f -> p s f")).then_inc(sGR[cp], 16)
            @blkB2.scalar
            def _(sc):
                b2_barrier(sc)

                def xo_load(n):
                    q, tb = n // 8, n % 8
                    sc.dma_start(xo_sb[n % 4][:, :],
                                 x_own[tb * 128:(tb + 1) * 128, q * 512:(q + 1) * 512]
                                 ).then_inc(sXO[n % 4], 16)

                for n0 in range(4):
                    xo_load(n0)
                for n in range(32):
                    q, tb = n // 8, n % 8
                    sc.wait_ge(sDVo, n + 1)
                    if n >= 2:
                        sc.wait_ge(sOUT[tb % 2], 16 * (n // 2))
                    sc.dma_start(out[tb * 128:(tb + 1) * 128, q * 512:(q + 1) * 512],
                                 out_sb[tb % 2][:, :]).then_inc(sOUT[tb % 2], 16)
                    if n + 4 < 32:
                        xo_load(n + 4)
                sc.wait_ge(sOUT[0], 16 * 16)
                sc.wait_ge(sOUT[1], 16 * 16)

            @blkB2.tensor
            def _(te):
                for q in range(4):
                    for grp in range(16):
                        n = 16 * q + grp
                        cp = n % 3
                        te.wait_ge(sGin[cp], 16 * (n // 3 + 1))
                        te.wait_ge(sGR[cp], 16 * (n // 3 + 1))
                        for sub in range(4):
                            for tb in range(8):
                                if grp == 0 and sub == 0 and q >= 1:
                                    te.wait_ge(sDVo, 8 * (q - 1) + tb + 1)
                                ins = te.matmul(banks[tb],
                                                gatedin[cp][:, sub * 1024 + tb * 128: sub * 1024 + (tb + 1) * 128],
                                                wd_sb[cp][:, sub * 512:(sub + 1) * 512],
                                                start=(grp == 0 and sub == 0), stop=False)
                        ins.then_inc(sPEhc, 1)
                    te.wait_ge(sWp[q % 2], 16 * (q // 2 + 1))
                    if q == 0:
                        te.wait_ge(sTctx, 16 * NDC)
                    for tb in range(8):
                        for hc2 in range(NDC):
                            ins = te.matmul(banks[tb],
                                            ctxT_sb[:, hc2 * OWN + tb * 128: hc2 * OWN + (tb + 1) * 128],
                                            wpq_sb[q % 2][:, hc2 * 512:(hc2 + 1) * 512],
                                            start=False, stop=(hc2 == NDC - 1))
                        ins.then_inc(sPEob, 1)

            @blkB2.vector
            def _(v):
                v.wait_ge(sBB, 16)

                def tt1(n):
                    q, tb = n // 8, n % 8
                    tp = tb % 2
                    v.wait_ge(sXO[n % 4], 16 * (n // 4 + 1))
                    if n >= 2:
                        v.wait_ge(sOUT[tp], 16 * (n // 2))
                    v.tensor_tensor(out_sb[tp][:, :], xo_sb[n % 4][:, :],
                                    bias_sb[:, q * 512:(q + 1) * 512], ALU.add)
                    v.drain()

                tt1(0)
                tt1(1)
                for n in range(32):
                    q, tb = n // 8, n % 8
                    tp = tb % 2
                    v.wait_ge(sPEob, n + 1)
                    v.tensor_tensor(out_sb[tp][:, :], out_sb[tp][:, :],
                                    banks[tb], ALU.add).then_inc(sDVo, 1)
                    v.drain()
                    if n + 2 < 32:
                        tt1(n + 2)

    return nc


def kernel(L0_attn_query, L0_attn_key, L0_attn_value, L0_attn_Wqkv, L0_attn_Wproj,
           L0_attn_bproj, L0_ffn_w_gate, L0_ffn_w_up, L0_ffn_w_down):
    x = np.ascontiguousarray(np.asarray(L0_attn_query, dtype=np.float32).reshape(NT, D))
    Wqkv = np.asarray(L0_attn_Wqkv, dtype=np.float32)
    Wproj = np.asarray(L0_attn_Wproj, dtype=np.float32)
    bproj = np.asarray(L0_attn_bproj, dtype=np.float32)
    w_gate = np.asarray(L0_ffn_w_gate, dtype=np.float32)
    w_up = np.asarray(L0_ffn_w_up, dtype=np.float32)
    w_down = np.asarray(L0_ffn_w_down, dtype=np.float32)

    Wr = Wqkv.reshape(NH, 3, 128, D)
    wq = Wr[:, 0].reshape(NH * 128, D)
    wk = Wr[:, 1].reshape(NH * 128, D)
    wv = Wr[:, 2].reshape(NH * 128, D)

    def per_pass(w):
        return np.ascontiguousarray(
            w.T.reshape(16, 128, NP, HP * 128).transpose(2, 1, 0, 3)).reshape(NP, 128, HP * 2048)

    ins = dict(
        x=x,
        wqp=per_pass(wq), wkp=per_pass(wk), wvp=per_pass(wv),
        wgb=np.ascontiguousarray(
            w_gate.T.reshape(16, 128, 64, 128).transpose(2, 1, 0, 3)).reshape(64, 128, 2048),
        wub=np.ascontiguousarray(
            w_up.T.reshape(16, 128, 64, 128).transpose(2, 1, 0, 3)).reshape(64, 128, 2048),
        wdq=np.ascontiguousarray(w_down.T.reshape(64, 128, 4, 512).transpose(2, 0, 1, 3)),
        wpq=np.ascontiguousarray(
            Wproj.T.reshape(16, 128, 4, 512).transpose(2, 1, 0, 3)).reshape(4, 128, 8192),
        biasb=np.ascontiguousarray(np.broadcast_to(bproj[None, :], (128, D))),
    )

    nc = build_nc()
    in_maps = []
    for c in range(8):
        m = dict(ins)
        m["x_own"] = np.ascontiguousarray(x[c * OWN:(c + 1) * OWN])
        in_maps.append(m)
    res = run_bass_kernel_spmd(nc, in_maps, core_ids=list(range(8)))
    full = np.concatenate([res.results[c]["out"] for c in range(8)], axis=0)
    return full.reshape(1, NT, D)
